# version 07
# baseline (speedup 1.0000x reference)
import numpy as np
import ml_dtypes
from contextlib import ExitStack

import concourse.bass as bass
import concourse.mybir as mybir
from concourse.bass_utils import run_bass_kernel_spmd

F32 = mybir.dt.float32
BF16 = mybir.dt.bfloat16
AF = mybir.ActivationFunctionType
ALU = mybir.AluOpType
AX = mybir.AxisListType

S = 4096
DM = 2048
NH = 16
HD = 128
DH = 8192
NTB = S // 128
NE8 = S // 512
SCALE = 1.0 / float(np.sqrt(HD))
EPS = 1e-5


class Sem:
    def __init__(self, h):
        self.h = h
        self.n = 0

    def bump(self, k=1):
        self.n += k
        return self.n


class Plan:
    ENGS = ("sync", "scalar", "vector", "tensor", "gpsimd")

    def __init__(self):
        self.q = {e: [] for e in self.ENGS}

    def add(self, eng, fn):
        self.q[eng].append(fn)

    def wait(self, eng, sem, val):
        if val > 0:
            self.add(eng, lambda e, s=sem.h, v=val: e.wait_ge(s, v))


def build_nc(phases=("A", "B", "C", "P1", "P2"), debug_out=(),
             b_heads=NH, b_flash=True, b_dump=False):
    nc = bass.Bass("TRN2", target_bir_lowering=False)

    x_d = nc.dram_tensor("x", [S, DM], F32, kind="ExternalInput").ap()
    wqkv_d = nc.dram_tensor("wqkv", [NH, 128, 3, 16, 128], F32, kind="ExternalInput").ap()
    wo_d = nc.dram_tensor("wo", [128, 16, DM], F32, kind="ExternalInput").ap()
    wg_d = nc.dram_tensor("wg", [64, 128, 16, 128], F32, kind="ExternalInput").ap()
    wu_d = nc.dram_tensor("wu", [64, 128, 16, 128], F32, kind="ExternalInput").ap()
    wd_d = nc.dram_tensor("wd", [2, 128, 64, 1024], F32, kind="ExternalInput").ap()
    cosx_d = nc.dram_tensor("cosx", [128, S], F32, kind="ExternalInput").ap()
    sinx_d = nc.dram_tensor("sinx", [128, S], F32, kind="ExternalInput").ap()
    pmat_d = nc.dram_tensor("pmat", [128, 128], BF16, kind="ExternalInput").ap()
    ident_d = nc.dram_tensor("ident", [128, 128], BF16, kind="ExternalInput").ap()
    masks_d = nc.dram_tensor("masks", [4, 128, 512], BF16, kind="ExternalInput").ap()

    xnt_d = nc.dram_tensor("xnt", [NE8, 128, 16, 512], BF16, kind="Internal").ap()
    yt_d = nc.dram_tensor("yt", [NH, 128, S], BF16, kind="Internal").ap()
    xp_d = nc.dram_tensor("xp", [S, DM], F32, kind="Internal").ap()
    gated_d = nc.dram_tensor("gated", [64, 128, 8, 512], BF16, kind="Internal").ap()

    out_d = nc.dram_tensor("out", [S, DM], F32, kind="ExternalOutput").ap()

    P = Plan()

    with ExitStack() as gctx:
        def g_sb(name, shape, dt):
            return gctx.enter_context(nc.sbuf_tensor(name, shape, dt))

        def g_ps(name, shape, dt):
            return gctx.enter_context(nc.psum_tensor(name, shape, dt))

        def g_sem(name):
            return Sem(gctx.enter_context(nc.semaphore(name)))


        c_eps = g_sb("c_eps", [128, 1], F32)
        c_nh = g_sb("c_nh", [128, 1], F32)
        c_inv = g_sb("c_inv", [128, 1], F32)
        c_neg1 = g_sb("c_neg1", [128, 1], F32)
        c_scale = g_sb("c_scale", [128, 1], F32)
        onescol = g_sb("onescol", [128, 1], BF16)
        onesrow = g_sb("onesrow", [128, 128], F32)
        ident = g_sb("ident_sb", [128, 128], BF16)
        pmat = g_sb("pmat_sb", [128, 128], BF16)

        sSET = g_sem("sSET")
        sSETd = g_sem("sSETd")
        sBAR = g_sem("sBAR")
        chA = g_sem("chA"); chD = g_sem("chD"); chP = g_sem("chP")
        sDMP = g_sem("sDMP")
        sAx = [g_sem(f"sAx{i}") for i in range(4)]
        sAw = [g_sem("sAw0"), g_sem("sAw1")]
        sAa = g_sem("sAa"); sAd = g_sem("sAd"); sAp = g_sem("sAp")
        sTAB = g_sem("sTAB")
        sBw = [g_sem("sBw0"), g_sem("sBw1")]
        sBx = [g_sem("sBx0"), g_sem("sBx1")]
        sBpe = g_sem("sBpe")
        sBev = g_sem("sBev"); sBrq = g_sem("sBrq"); sBrk = g_sem("sBrk")
        sBvt = g_sem("sBvt"); sBve = g_sem("sBve")
        sBsc = g_sem("sBsc"); sBex = g_sem("sBex"); sBmk = g_sem("sBmk")
        sBavl = g_sem("sBavl"); sBle = g_sem("sBle"); sBlr = g_sem("sBlr")
        sBrb = g_sem("sBrb"); sBrc = g_sem("sBrc"); sByn = g_sem("sByn")
        sBoe = g_sem("sBoe")
        sByw = [g_sem("sByw0"), g_sem("sByw1")]
        sCw = g_sem("sCw")
        sCy = [g_sem("sCy0"), g_sem("sCy1")]
        sCx = [g_sem("sCx0"), g_sem("sCx1")]
        sCmm = g_sem("sCmm"); sCev = g_sem("sCev")
        sCxw = [g_sem("sCxw0"), g_sem("sCxw1")]
        s1x = [g_sem(f"s1x{i}") for i in range(4)]
        s1a = g_sem("s1a"); s1d = g_sem("s1d"); s1p = g_sem("s1p")
        s1w = [g_sem("s1w0"), g_sem("s1w1")]
        s1mm = g_sem("s1mm"); s1si = g_sem("s1si"); s1mu = g_sem("s1mu")
        s1gw = [g_sem("s1gw0"), g_sem("s1gw1")]
        s2wd = [g_sem(f"s2wd{i}") for i in range(8)]
        s2g = [g_sem(f"s2g{i}") for i in range(4)]
        s2x = [g_sem("s2x0"), g_sem("s2x1")]
        s2mm = g_sem("s2mm"); s2ev = g_sem("s2ev")
        s2ow = [g_sem("s2ow0"), g_sem("s2ow1")]

        def plan_setup():
            def f(v):
                v.memset(c_eps[:], EPS)
                v.memset(c_nh[:], -0.5)
                v.memset(c_inv[:], 1.0 / DM)
                v.memset(c_neg1[:], -1.0)
                v.memset(c_scale[:], SCALE)
                v.memset(onescol[:], 1.0)
                v.memset(onesrow[:], 1.0).then_inc(sSET.h, 1)
            P.add("vector", f)
            sSET.bump()

            def fg(gp):
                gp.dma_start(ident[:], ident_d).then_inc(sSETd.h, 16)
                gp.dma_start(pmat[:], pmat_d).then_inc(sSETd.h, 16)
            P.add("gpsimd", fg)
            sSETd.bump(32)

        def barrier():
            base = sBAR.n
            for eng in Plan.ENGS:
                P.add(eng, lambda e: e.nop().then_inc(sBAR.h, 1))
            sBAR.bump(5)
            tgt = base + 5
            for eng in Plan.ENGS:
                P.wait(eng, sBAR, tgt)

        def mk_stats(ctx, tag, n=4):
            out = []
            for p in range(n):
                d = {}
                for nm in ("acc1", "acc2", "mean", "tmp", "var",
                           "rstd", "nmr"):
                    d[nm] = ctx.enter_context(
                        nc.sbuf_tensor(f"{tag}_{nm}{p}", [128, 1], F32))
                out.append(d)
            return out

        def mk_ln(xa, xn, stats, scr1, scr2, sa, sd, D=4):
            M = dict(a_sum={}, d_sq={}, d_var={}, a_rstd={}, d_nmr={},
                     norm={}, x_rdy={})

            def s_fa1(i):
                st = stats[i % D]
                P.wait("scalar", M["sx"][i % D], M["x_rdy"][i])
                if i >= D:
                    P.wait("scalar", sd, M["d_var"][i - D])
                ca = chA.bump()
                def f(s, xa=xa[i % D], a1=st["acc1"], ca=ca):
                    s.memzero(a1[:]).then_inc(chA.h, 1)
                    s.wait_ge(chA.h, ca)
                    s.activation(scr1[:], xa[:], AF.Copy,
                                 accum_out=a1[:, 0:1]).then_inc(sa.h, 1)
                P.add("scalar", f)
                M["a_sum"][i] = sa.bump()

            def s_fd1(i):
                st = stats[i % D]
                P.wait("vector", M["sx"][i % D], M["x_rdy"][i])
                cd = chD.bump()
                def f(v, xa=xa[i % D], a2=st["acc2"], cd=cd):
                    v.memset(a2[:], 0.0).then_inc(chD.h, 1)
                    v.wait_ge(chD.h, cd)
                    v.scalar_tensor_tensor(scr2[:], xa[:], 1.0, xa[:],
                                           ALU.mult, ALU.mult,
                                           accum_out=a2[:, 0:1]
                                           ).then_inc(sd.h, 1)
                P.add("vector", f)
                M["d_sq"][i] = sd.bump()

            def s_fd2(i):
                st = stats[i % D]
                P.wait("vector", sa, M["a_sum"][i])
                if i >= D:
                    P.wait("vector", sa, M["a_rstd"][i - D])
                cd1 = chD.bump(); cd2 = chD.bump()
                def f(v, st=st, cd1=cd1, cd2=cd2):
                    v.tensor_scalar_mul(st["mean"][:], st["acc1"][:],
                                        c_inv[:, 0:1]).then_inc(chD.h, 1)
                    v.wait_ge(chD.h, cd1)
                    v.tensor_mul(st["tmp"][:], st["mean"][:],
                                 st["mean"][:]).then_inc(chD.h, 1)
                    v.wait_ge(chD.h, cd2)
                    v.scalar_tensor_tensor(st["var"][:], st["acc2"][:],
                                           c_inv[:, 0:1], st["tmp"][:],
                                           ALU.mult, ALU.subtract
                                           ).then_inc(sd.h, 1)
                P.add("vector", f)
                M["d_var"][i] = sd.bump()

            def s_fa2(i):
                st = stats[i % D]
                P.wait("scalar", sd, M["d_var"][i])
                if i >= D:
                    P.wait("scalar", sd, M["d_nmr"][i - D])
                ca2 = chA.bump()
                def f(s, st=st, ca2=ca2):
                    s.activation(st["tmp"][:], st["var"][:], AF.Ln,
                                 bias=c_eps[:, 0:1]).then_inc(chA.h, 1)
                    s.wait_ge(chA.h, ca2)
                    s.activation(st["rstd"][:], st["tmp"][:], AF.Exp,
                                 scale=c_nh[:, 0:1]).then_inc(sa.h, 1)
                P.add("scalar", f)
                M["a_rstd"][i] = sa.bump()

            def s_fd3(i):
                st = stats[i % D]
                P.wait("vector", sa, M["a_rstd"][i])
                if i >= D:
                    P.wait("vector", sa, M["norm"][i - D])
                def f(v, st=st):
                    v.scalar_tensor_tensor(st["nmr"][:], st["mean"][:],
                                           c_neg1[:, 0:1], st["rstd"][:],
                                           ALU.mult, ALU.mult
                                           ).then_inc(sd.h, 1)
                P.add("vector", f)
                M["d_nmr"][i] = sd.bump()

            def s_fa3(i):
                st = stats[i % D]
                P.wait("scalar", sd, M["d_nmr"][i])
                def f(s, xa=xa[i % D], xn=xn[i % D], st=st):
                    s.activation(xn[:], xa[:], AF.Identity,
                                 bias=st["nmr"][:, 0:1],
                                 scale=st["rstd"][:, 0:1]).then_inc(sa.h, 1)
                P.add("scalar", f)
                M["norm"][i] = sa.bump()

            return M, s_fa1, s_fd1, s_fd2, s_fa2, s_fd3, s_fa3

        def plan_A():
            with ExitStack() as ctx:
                sb = lambda n, sh, dt: ctx.enter_context(nc.sbuf_tensor(n, sh, dt))
                ps = lambda n, sh, dt: ctx.enter_context(nc.psum_tensor(n, sh, dt))
                xa = [sb(f"A_xa{p}", [128, DM], F32) for p in range(4)]
                xn = [sb(f"A_xn{p}", [128, DM], BF16) for p in range(4)]
                scr1 = sb("A_scr1", [128, DM], F32)
                scr2 = sb("A_scr2", [128, DM], F32)
                stage = [sb(f"A_stg{p}", [128, 8192], BF16) for p in range(2)]
                stats = mk_stats(ctx, "A")
                pT2 = [ps(f"A_pT{p}", [128, 1024], BF16) for p in range(2)]

                L, s_fa1, s_fd1, s_fd2, s_fa2, s_fd3, s_fa3 = mk_ln(
                    xa, xn, stats, scr1, scr2, sAa, sAd)
                L["sx"] = sAx

                ev_done = [0] * NTB
                st_tr = {"tr_cnt": 0}
                ev_cnt_hist = []
                wr_done = [0] * NE8

                def s_load(i):
                    pi = i % 4
                    if i >= 4:
                        P.wait("sync", sAa, L["norm"][i - 4])
                    def fl(sy, i=i, pi=pi):
                        sy.dma_start(xa[pi][:], x_d[i * 128:(i + 1) * 128, :]
                                     ).then_inc(sAx[pi].h, 16)
                    P.add("sync", fl)
                    L["x_rdy"][i] = sAx[pi].bump(16)

                def s_tail(k):
                    if k >= 4:
                        P.wait("scalar", sAp, (k - 4 + 1) * 16)
                    s_fa3(k)
                    pi = k % 4
                    e8 = k // 4
                    slot = k % 4
                    gpar = e8 % 2
                    P.wait("tensor", sAa, L["norm"][k])
                    if e8 >= 2 and slot == 0:
                        P.wait("vector", sAw[gpar], wr_done[e8 - 2])
                    for grp in range(2):
                        tr_cnt = st_tr["tr_cnt"]
                        bk = (tr_cnt // 8) % 2
                        if tr_cnt >= 16:
                            P.wait("tensor", sAd, ev_cnt_hist[tr_cnt // 8 - 2])
                        for sl in range(8):
                            dc = grp * 8 + sl
                            def ft(t, pi=pi, dc=dc, sl=sl, bk=bk):
                                t.transpose(pT2[bk][:, sl * 128:(sl + 1) * 128],
                                            xn[pi][:, dc * 128:(dc + 1) * 128],
                                            ident[:]).then_inc(sAp.h, 1)
                            P.add("tensor", ft)
                            sAp.bump()
                            st_tr["tr_cnt"] += 1
                        P.wait("vector", sAp, sAp.n)
                        def fe(v, gpar=gpar, grp=grp, slot=slot, bk=bk):
                            ins = None
                            for sl in range(8):
                                dc = grp * 8 + sl
                                ins = v.tensor_copy(
                                    stage[gpar][:, dc * 512 + slot * 128:
                                                dc * 512 + slot * 128 + 128],
                                    pT2[bk][:, sl * 128:(sl + 1) * 128])
                            ins.then_inc(sAd.h, 1)
                        P.add("vector", fe)
                        ev_cnt_hist.append(sAd.bump())
                    ev_done[k] = sAd.n
                    if slot == 3:
                        P.wait("sync", sAd, ev_done[k])
                        def fw(sy, e8=e8, gpar=gpar):
                            src = stage[gpar][:].rearrange(
                                "p (dc c) -> p dc c", c=512)
                            sy.dma_start(xnt_d[e8], src).then_inc(sAw[gpar].h, 16)
                        P.add("sync", fw)
                        wr_done[e8] = sAw[gpar].bump(16)

                for i in range(NTB + 2):
                    if i < NTB:
                        s_load(i)
                        s_fa1(i)
                        s_fd1(i)
                    j = i - 1
                    if 0 <= j < NTB:
                        s_fd2(j)
                        s_fa2(j)
                    k = i - 2
                    if 0 <= k < NTB:
                        s_fd3(k)
                        s_tail(k)
                P.wait("sync", sAw[0], sAw[0].n)
                P.wait("sync", sAw[1], sAw[1].n)

        def plan_B():
            with ExitStack() as ctx:
                sb = lambda n, sh, dt: ctx.enter_context(nc.sbuf_tensor(n, sh, dt))
                ps = lambda n, sh, dt: ctx.enter_context(nc.psum_tensor(n, sh, dt))
                pSA = ps("B_pSA", [128, 1024], F32)
                pSB = ps("B_pSB", [128, 1024], F32)
                pO0 = ps("B_pO0", [128, 512], F32)
                pO1 = ps("B_pO1", [128, 512], F32)
                pL = ps("B_pL", [128, 512], F32)
                pTb = ps("B_pTb", [128, 1024], BF16)
                xc = [sb(f"B_xc{p}", [128, 8192], BF16) for p in range(2)]
                wq = [sb(f"B_wq{p}", [128, 6144], BF16) for p in range(2)]
                q_pre = [sb(f"B_qp{p}", [128, 512], BF16) for p in range(2)]
                k_pre = [sb(f"B_kp{p}", [128, 512], BF16) for p in range(2)]
                v_pre = [sb(f"B_vp{p}", [128, 512], BF16) for p in range(2)]
                qsq = [sb(f"B_qsq{p}", [128, 512], BF16) for p in range(2)]
                qsk = [sb(f"B_qsk{p}", [128, 512], BF16) for p in range(2)]
                q_rot = [sb(f"B_qr{p}", [128, S], BF16) for p in range(2)]
                k_rot = [sb(f"B_kr{p}", [128, S], BF16) for p in range(2)]
                v_sb = [sb(f"B_vs{p}", [128, S], BF16) for p in range(2)]
                cosx = sb("B_cosx", [128, S], F32)
                sinx = sb("B_sinx", [128, S], F32)
                tq1 = sb("B_tq1", [128, 512], F32)
                tq2 = sb("B_tq2", [128, 512], F32)
                tk1 = sb("B_tk1", [128, 512], F32)
                tk2 = sb("B_tk2", [128, 512], F32)
                pT = [sb(f"B_pT{p}", [128, 1024], BF16) for p in range(3)]
                masks = sb("B_masks", [128, 2048], BF16)
                oT_sb = [sb(f"B_oT{q}", [128, 1024], F32) for q in range(4)]
                l_sb = [[sb(f"B_l{j}{g}", [128, 512], F32) for g in range(2)]
                        for j in range(2)]
                r_sb = [[sb(f"B_r{j}{g}", [128, 512], F32) for g in range(2)]
                        for j in range(2)]
                rb_sb = [sb(f"B_rb{p}", [128, 512], F32) for p in range(2)]
                y_st = [sb(f"B_y{p}", [128, 512], BF16) for p in range(2)]

                def f0(gp):
                    gp.dma_start(cosx[:], cosx_d).then_inc(sTAB.h, 16)
                    gp.dma_start(sinx[:], sinx_d).then_inc(sTAB.h, 16)
                    gp.dma_start(masks[:].rearrange("p (o c) -> p o c", c=512),
                                 masks_d.rearrange("o p c -> p o c")
                                 ).then_inc(sTAB.h, 16)
                P.add("gpsimd", f0)
                sTAB.bump(48)
                for eng in ("tensor", "vector", "gpsimd", "scalar"):
                    P.wait(eng, sTAB, 48)

                ev_q = {}; ev_k = {}; ev_v = {}; ev_qsq = {}; ev_qsk = {}
                rq_done = {}; rk_done = {}; ve_done = {}
                pe_xc_done = {}
                pe_wq_done = {}
                flash_q_done = {}
                flash_av_done = {}
                wq_req = {}

                def plan_B1(h, interleave=()):
                    interleave = list(interleave)
                    ph2 = h % 2
                    def fw(gp, h=h, ph2=ph2, need=pe_wq_done.get(h - 2, 0)):
                        if need:
                            gp.wait_ge(sBpe.h, need)
                        gp.dma_start(wq[ph2][:].rearrange(
                                         "p (t dc f) -> p t dc f", t=3, dc=16),
                                     wqkv_d[h]).then_inc(sBw[ph2].h, 16)
                    P.add("gpsimd", fw)
                    wq_req[h] = sBw[ph2].bump(16)

                    def plan_qkv(e8):
                        ge = h * NE8 + e8
                        pe2 = ge % 2
                        def fx(sy, e8=e8, pe2=pe2,
                               need=pe_xc_done.get(ge - 2, 0)):
                            if need:
                                sy.wait_ge(sBpe.h, need)
                            dst = xc[pe2][:].rearrange(
                                "p (dc c) -> p dc c", c=512)
                            sy.dma_start(dst, xnt_d[e8]).then_inc(sBx[pe2].h, 16)
                        P.add("sync", fx)
                        x_rdy = sBx[pe2].bump(16)

                        P.wait("tensor", sBx[pe2], x_rdy)
                        P.wait("tensor", sBw[ph2], wq_req[h])
                        if ge > 0:
                            P.wait("tensor", sBev, ev_v.get(ge - 1, 0))
                        P.wait("tensor", sBex, sBex.n)
                        qkv_dst = ((0, pSA, 0), (1, pSA, 512), (2, pSB, 0))
                        for trio, acc, c0 in qkv_dst:
                            for dc in range(16):
                                def fm(t, pe2=pe2, dc=dc, trio=trio,
                                       acc=acc, c0=c0):
                                    ins = t.matmul(
                                        acc[:, c0:c0 + 512],
                                        wq[ph2][:, (trio * 16 + dc) * 128:
                                                (trio * 16 + dc) * 128 + 128],
                                        xc[pe2][:, dc * 512:(dc + 1) * 512],
                                        start=(dc == 0), stop=(dc == 15))
                                    if dc == 15:
                                        ins.then_inc(sBpe.h, 1)
                                P.add("tensor", fm)
                        m_q = sBpe.n + 1
                        m_k = sBpe.n + 2
                        m_v = sBpe.bump(3)
                        qkv_done = m_v
                        pe_xc_done[ge] = m_v
                        if e8 == NE8 - 1:
                            pe_wq_done[h] = m_v

                        P.wait("scalar", sBpe, m_q)
                        if ge >= 2:
                            P.wait("scalar", sBrq, rq_done.get(ge - 2, 0))
                        def fe_q(s, pe2=pe2):
                            s.copy(q_pre[pe2][:], pSA[:, 0:512]).then_inc(sBev.h, 1)
                        P.add("scalar", fe_q)
                        ev_q[ge] = sBev.bump()
                        P.wait("scalar", sBpe, m_k)
                        if ge >= 2:
                            P.wait("scalar", sBrk, rk_done.get(ge - 2, 0))
                        def fe_k(s, pe2=pe2):
                            s.copy(k_pre[pe2][:], pSA[:, 512:1024]).then_inc(sBev.h, 1)
                        P.add("scalar", fe_k)
                        ev_k[ge] = sBev.bump()
                        P.wait("scalar", sBpe, m_v)
                        if ge >= 2:
                            P.wait("scalar", sBvt, 4 * (ge - 1))
                        def fe_v(s, pe2=pe2):
                            s.copy(v_pre[pe2][:], pSB[:, 0:512]).then_inc(sBev.h, 1)
                        P.add("scalar", fe_v)
                        ev_v[ge] = sBev.bump()

                    def plan_tail(e8):
                        ge = h * NE8 + e8
                        pe2 = ge % 2
                        P.wait("tensor", sBev, ev_q[ge])
                        if ge >= 1:
                            P.wait("tensor", sBev, ev_qsq.get(ge - 1, 0))
                        if e8 == 0:
                            P.wait("tensor", sBoe, sBoe.n)
                        def fsq(t, pe2=pe2):
                            t.matmul(pSB[:, 512:1024], pmat[:], q_pre[pe2][:],
                                     start=True, stop=True).then_inc(sBpe.h, 1)
                        P.add("tensor", fsq)
                        P.wait("tensor", sBev, ev_k[ge])
                        if ge >= 1:
                            P.wait("tensor", sBev, ev_qsk.get(ge - 1, 0))
                        def fsk(t, pe2=pe2):
                            t.matmul(pO0[:], pmat[:], k_pre[pe2][:],
                                     start=True, stop=True).then_inc(sBpe.h, 1)
                        P.add("tensor", fsk)
                        qs_done = sBpe.bump(2)
                        P.wait("tensor", sBev, ev_v[ge])
                        if ge >= 1:
                            P.wait("tensor", sBve, ve_done.get(ge - 1, 0))
                        for j in range(4):
                            def ftr(t, pe2=pe2, j=j):
                                t.transpose(pTb[:, j * 128:(j + 1) * 128],
                                            v_pre[pe2][:, j * 128:(j + 1) * 128],
                                            ident[:]).then_inc(sBvt.h, 1)
                            P.add("tensor", ftr)
                        sBvt.bump(4)
                        vt_done = sBvt.n

                        P.wait("scalar", sBpe, qs_done)
                        def feq(s, pe2=pe2):
                            s.copy(qsq[pe2][:], pSB[:, 512:1024]).then_inc(sBev.h, 1)
                            s.copy(qsk[pe2][:], pO0[:]).then_inc(sBev.h, 1)
                        P.add("scalar", feq)
                        ev_qsq[ge] = sBev.n + 1
                        ev_qsk[ge] = sBev.bump(2)

                        P.wait("gpsimd", sBev, ev_qsq[ge])
                        P.wait("gpsimd", sBrq, sBrq.n)
                        if h >= 2 and e8 == 0:
                            P.wait("gpsimd", sBsc, flash_q_done.get(h - 2, 0))
                        cq = chP.bump(2)
                        def frq(gp, pe2=pe2, e8=e8, cq=cq):
                            ts = e8 * 512
                            gp.tensor_mul(tq1[:], q_pre[pe2][:],
                                          cosx[:, ts:ts + 512]).then_inc(chP.h, 1)
                            gp.tensor_mul(tq2[:], qsq[pe2][:],
                                          sinx[:, ts:ts + 512]).then_inc(chP.h, 1)
                            gp.wait_ge(chP.h, cq)
                            gp.tensor_add(q_rot[ph2][:, ts:ts + 512],
                                          tq1[:], tq2[:]).then_inc(sBrq.h, 1)
                        P.add("gpsimd", frq)
                        rq_done[ge] = sBrq.bump()

                        P.wait("gpsimd", sBev, ev_qsk[ge])
                        P.wait("gpsimd", sBrk, sBrk.n)
                        if h >= 2 and e8 == 0:
                            P.wait("gpsimd", sBsc, flash_q_done.get(h - 2, 0))
                        ck = chP.bump(2)
                        def frk(gp, pe2=pe2, e8=e8, ck=ck):
                            ts = e8 * 512
                            gp.tensor_mul(tk1[:], k_pre[pe2][:],
                                          cosx[:, ts:ts + 512]).then_inc(chP.h, 1)
                            gp.tensor_mul(tk2[:], qsk[pe2][:],
                                          sinx[:, ts:ts + 512]).then_inc(chP.h, 1)
                            gp.wait_ge(chP.h, ck)
                            gp.tensor_add(k_rot[ph2][:, ts:ts + 512],
                                          tk1[:], tk2[:]).then_inc(sBrk.h, 1)
                        P.add("gpsimd", frk)
                        rk_done[ge] = sBrk.bump()

                        P.wait("vector", sBvt, vt_done)
                        if h >= 2 and e8 == 0:
                            P.wait("vector", sBavl, flash_av_done.get(h - 2, 0))
                        def fve(v, pe2=pe2, e8=e8):
                            for j in range(4):
                                ins = v.tensor_copy(
                                    v_sb[ph2][:, e8 * 512 + j * 128:
                                              e8 * 512 + j * 128 + 128],
                                    pTb[:, j * 128:(j + 1) * 128])
                            ins.then_inc(sBve.h, 1)
                        P.add("vector", fve)
                        ve_done[ge] = sBve.bump()

                    for e8 in range(NE8):
                        plan_qkv(e8)
                        if e8 >= 1:
                            plan_tail(e8 - 1)
                        if interleave:
                            interleave.pop(0)()
                    plan_tail(NE8 - 1)
                    for em in interleave:
                        em()

                def plan_flash(h):
                    ph2 = h % 2
                    P.wait("tensor", sBrq, rq_done[h * NE8 + NE8 - 1])
                    P.wait("tensor", sBrk, rk_done[h * NE8 + NE8 - 1])
                    P.wait("tensor", sBve, ve_done[h * NE8 + NE8 - 1])
                    P.wait("tensor", sBev, sBev.n)

                    yw_hist = getattr(plan_flash, "_yw_hist", {0: 0, 1: 0})
                    plan_flash._yw_hist = yw_hist
                    le_hist = {}
                    scb = (pSA, pSB)
                    for qq in range(4):
                        n_kt = 8 * qq + 8
                        q0 = qq * 1024
                        exp_hist = {}
                        mk_hist = {}
                        avl_hist = {}
                        P.wait("tensor", sBex, sBex.n)

                        def emit_scores(kt):
                            par = kt % 2
                            both = kt < 8 * qq + 4
                            sc = scb[par]
                            if kt >= 2:
                                P.wait("tensor", sBex, exp_hist[kt - 2])
                            def fs(t, kt=kt, sc=sc, both=both, q0=q0):
                                if both:
                                    t.matmul(sc[:, 0:512],
                                             k_rot[ph2][:, kt * 128:(kt + 1) * 128],
                                             q_rot[ph2][:, q0:q0 + 512],
                                             start=True, stop=True
                                             ).then_inc(sBsc.h, 1)
                                t.matmul(sc[:, 512:1024],
                                         k_rot[ph2][:, kt * 128:(kt + 1) * 128],
                                         q_rot[ph2][:, q0 + 512:q0 + 1024],
                                         start=True, stop=True
                                         ).then_inc(sBsc.h, 1)
                            P.add("tensor", fs)
                            return sBsc.bump(2 if both else 1)

                        def emit_exp(kt, sc_done):
                            par = kt % 2
                            p3 = kt % 3
                            both = kt < 8 * qq + 4
                            sc = scb[par]
                            P.wait("scalar", sBsc, sc_done)
                            if kt >= 3:
                                P.wait("scalar", sBavl, avl_hist[kt - 3])
                            def fx(s, p3=p3, sc=sc, both=both):
                                if both:
                                    s.activation(pT[p3][:], sc[:], AF.Exp,
                                                 scale=c_scale[:, 0:1]
                                                 ).then_inc(sBex.h, 1)
                                else:
                                    s.activation(pT[p3][:, 512:1024],
                                                 sc[:, 512:1024], AF.Exp,
                                                 scale=c_scale[:, 0:1]
                                                 ).then_inc(sBex.h, 1)
                            P.add("scalar", fx)
                            exp_hist[kt] = sBex.bump()

                        def emit_mask(kt):
                            if kt < 8 * qq:
                                return
                            p3 = kt % 3
                            jdiag = (kt - 8 * qq) // 4
                            o = (kt - 8 * qq) % 4
                            P.wait("vector", sBex, exp_hist[kt])
                            def fm(v, p3=p3, jdiag=jdiag, o=o):
                                w = (o + 1) * 128
                                v.tensor_mul(
                                    pT[p3][:, jdiag * 512:jdiag * 512 + w],
                                    pT[p3][:, jdiag * 512:jdiag * 512 + w],
                                    masks[:, o * 512:o * 512 + w]
                                ).then_inc(sBmk.h, 1)
                            P.add("vector", fm)
                            mk_hist[kt] = sBmk.bump()

                        def emit_avl(kt):
                            p3 = kt % 3
                            both = kt < 8 * qq + 4
                            if kt == 0:
                                P.wait("tensor", sBoe, sBoe.n)
                                P.wait("tensor", sBle, sBle.n)
                                P.wait("tensor", sBrc, sBrc.n)
                            if kt in mk_hist:
                                P.wait("tensor", sBmk, mk_hist[kt])
                            else:
                                P.wait("tensor", sBex, exp_hist[kt])
                            js = [0, 1] if both else [1]
                            for j in js:
                                last = (kt == (8 * qq + 4 - 1)) if j == 0 \
                                    else (kt == n_kt - 1)
                                pO = pO0 if j == 0 else pO1
                                def fav(t, kt=kt, p3=p3, j=j,
                                        last=last, pO=pO):
                                    ins = t.matmul(
                                        pO[:],
                                        v_sb[ph2][:, kt * 128:(kt + 1) * 128],
                                        pT[p3][:, j * 512:(j + 1) * 512],
                                        start=(kt == 0), stop=last)
                                    if last:
                                        ins.then_inc(sBavl.h, 1)
                                P.add("tensor", fav)
                                if last:
                                    sBavl.bump()
                                def fl(t, kt=kt, p3=p3, j=j, last=last):
                                    t.matmul(
                                        pL[j * 32:j * 32 + 1, :],
                                        onescol[:],
                                        pT[p3][:, j * 512:(j + 1) * 512],
                                        start=(kt == 0), stop=last
                                    ).then_inc(sBavl.h, 1)
                                P.add("tensor", fl)
                                sBavl.bump()
                            avl_hist[kt] = sBavl.n

                        for kt in range(n_kt):
                            sc_done = emit_scores(kt)
                            emit_exp(kt, sc_done)
                            emit_mask(kt)
                            if kt >= 2:
                                emit_avl(kt - 2)
                        emit_avl(n_kt - 2)
                        emit_avl(n_kt - 1)

                        q_avl = sBavl.n
                        P.wait("scalar", sBavl, q_avl)
                        P.wait("scalar", sBrc, sBrc.n)
                        def fle(s, qq=qq):
                            g, b = qq // 2, (qq % 2) * 32
                            s.copy(l_sb[0][g][b:b + 1, :], pL[0:1, :])
                            s.copy(l_sb[1][g][b:b + 1, :], pL[32:33, :]
                                   ).then_inc(sBle.h, 1)
                        P.add("scalar", fle)
                        le_hist[qq] = sBle.bump()
                        P.wait("scalar", sBavl, q_avl)
                        P.wait("scalar", sByn, sByn.n)
                        def foe0(s, qq=qq):
                            s.copy(oT_sb[qq][:, 0:512], pO0[:]).then_inc(sBoe.h, 1)
                        P.add("scalar", foe0)
                        P.wait("vector", sBavl, q_avl)
                        P.wait("vector", sByn, sByn.n)
                        def foe1(v, qq=qq):
                            v.tensor_copy(oT_sb[qq][:, 512:1024], pO1[:]
                                          ).then_inc(sBoe.h, 1)
                        P.add("vector", foe1)
                        sBoe.bump(2)

                    P.wait("vector", sBle, sBle.n)
                    P.wait("vector", sBrb, sBrb.n)
                    def flr(v):
                        ins = None
                        for qq in range(4):
                            g, b = qq // 2, (qq % 2) * 32
                            ins = v.reciprocal(r_sb[0][g][b:b + 1, :],
                                               l_sb[0][g][b:b + 1, :])
                            ins = v.reciprocal(r_sb[1][g][b:b + 1, :],
                                               l_sb[1][g][b:b + 1, :])
                        ins.then_inc(sBlr.h, 1)
                    P.add("vector", flr)
                    lr_done = sBlr.bump()
                    yn_hist = {}

                    def mk_epi(si):
                        def emit():
                            qq, j = si // 2, si % 2
                            g, b = qq // 2, (qq % 2) * 32
                            P.wait("tensor", sBlr, lr_done)
                            if si >= 1:
                                P.wait("tensor", sBrc, sBrc.n)
                            if si >= 2:
                                P.wait("tensor", sByn, yn_hist[si - 2])
                            def frb(t, j=j, g=g, b=b):
                                t.matmul(pL[:], onesrow[b:b + 1, :],
                                         r_sb[j][g][b:b + 1, :], start=True,
                                         stop=True).then_inc(sBrb.h, 1)
                            P.add("tensor", frb)
                            rbn = sBrb.bump()
                            P.wait("vector", sBrb, rbn)
                            def frc(v, si=si):
                                v.tensor_copy(rb_sb[si % 2][:], pL[:]
                                              ).then_inc(sBrc.h, 1)
                            P.add("vector", frc)
                            rcn = sBrc.bump()
                            P.wait("vector", sBrc, rcn)
                            if yw_hist[si % 2]:
                                P.wait("vector", sByw[si % 2], yw_hist[si % 2])
                            def fyn(v, si=si, qq=qq, j=j):
                                v.tensor_mul(y_st[si % 2][:],
                                             oT_sb[qq][:, j * 512:(j + 1) * 512],
                                             rb_sb[si % 2][:]).then_inc(sByn.h, 1)
                            P.add("vector", fyn)
                            yn_hist[si] = sByn.bump()
                            P.wait("scalar", sByn, yn_hist[si])
                            def fyw(s, h=h, qq=qq, j=j, si=si):
                                c0 = qq * 1024 + j * 512
                                s.dma_start(yt_d[h][:, c0:c0 + 512],
                                            y_st[si % 2][:]).then_inc(
                                                sByw[si % 2].h, 16)
                            P.add("scalar", fyw)
                            yw_hist[si % 2] = sByw[si % 2].bump(16)
                        return emit

                    epi = [mk_epi(si) for si in range(8)]

                    flash_q_done[h] = sBsc.n
                    flash_av_done[h] = sBavl.n
                    return epi

                plan_B1(0)
                pend = []
                for h in range(b_heads):
                    if h + 1 < b_heads:
                        plan_B1(h + 1, interleave=pend)
                    else:
                        for em in pend:
                            em()
                    pend = plan_flash(h) if b_flash else []
                for em in pend:
                    em()
                P.wait("scalar", sByw[0], sByw[0].n)
                P.wait("scalar", sByw[1], sByw[1].n)
                if b_dump:
                    dbg_q = nc.dram_tensor("dbg_qrot", [128, S], BF16,
                                           kind="ExternalOutput").ap()
                    dbg_k = nc.dram_tensor("dbg_krot", [128, S], BF16,
                                           kind="ExternalOutput").ap()
                    dbg_v = nc.dram_tensor("dbg_vsb", [128, S], BF16,
                                           kind="ExternalOutput").ap()
                    hh = (b_heads - 1) % 2
                    P.wait("sync", sBrq, sBrq.n)
                    P.wait("sync", sBrk, sBrk.n)
                    P.wait("sync", sBve, sBve.n)
                    def fdmp(sy, hh=hh):
                        sy.dma_start(dbg_q, q_rot[hh][:]).then_inc(sDMP.h, 16)
                        sy.dma_start(dbg_k, k_rot[hh][:]).then_inc(sDMP.h, 16)
                        sy.dma_start(dbg_v, v_sb[hh][:]).then_inc(sDMP.h, 16)
                    P.add("sync", fdmp)
                    sDMP.bump(48)
                    P.wait("sync", sDMP, sDMP.n)

        def plan_C():
            with ExitStack() as ctx:
                sb = lambda n, sh, dt: ctx.enter_context(nc.sbuf_tensor(n, sh, dt))
                ps = lambda n, sh, dt: ctx.enter_context(nc.psum_tensor(n, sh, dt))
                pSA = ps("C_pA", [128, 1024], F32)
                pSB = ps("C_pB", [128, 1024], F32)
                wo_sb = sb("C_wo", [128, 32768], BF16)
                ytb = [sb(f"C_yt{p}", [128, 8192], BF16) for p in range(2)]
                xti = [sb(f"C_x{p}", [128, DM], F32) for p in range(2)]
                xps = [sb(f"C_xp{p}", [128, DM], F32) for p in range(2)]

                def fw(gp):
                    for c in range(4):
                        gp.dma_start(
                            wo_sb[:, c * 8192:(c + 1) * 8192].rearrange(
                                "p (dc f) -> p dc f", f=2048),
                            wo_d[:, c * 4:(c + 1) * 4, :]).then_inc(sCw.h, 16)
                P.add("gpsimd", fw)
                sCw.bump(64)

                mm_hist = {}
                ev_hist = {}
                for tg in range(8):
                    gpar = tg % 2
                    if tg >= 2:
                        P.wait("sync", sCmm, mm_hist[(tg - 2, 3)])
                    def fy(sy, tg=tg, gpar=gpar):
                        for hh in range(NH):
                            sy.dma_start(
                                ytb[gpar][:, hh * 512:(hh + 1) * 512],
                                yt_d[hh][:, tg * 512:(tg + 1) * 512]
                            ).then_inc(sCy[gpar].h, 16)
                    P.add("sync", fy)
                    y_rdy = sCy[gpar].bump(16 * NH)

                    for tb in range(4):
                        i = tg * 4 + tb
                        pi = i % 2
                        if i >= 2:
                            P.wait("sync", sCev, ev_hist[i - 2])
                        def flx(sy, i=i, pi=pi):
                            sy.dma_start(xti[pi][:],
                                         x_d[i * 128:(i + 1) * 128, :]
                                         ).then_inc(sCx[pi].h, 16)
                        P.add("sync", flx)
                        x_rdy = sCx[pi].bump(16)

                        P.wait("tensor", sCy[gpar], y_rdy)
                        P.wait("tensor", sCw, 64)
                        for half, acc in ((0, pSA), (1, pSB)):
                            if i >= 1:
                                P.wait("tensor", sCev, ev_hist.get(i - 1, 0))
                            for hh in range(NH):
                                def fmm(t, gpar=gpar, tb=tb, hh=hh, half=half,
                                        acc=acc):
                                    lh = ytb[gpar][:, hh * 512 + tb * 128:
                                                   hh * 512 + tb * 128 + 128]
                                    base = hh * 2048 + half * 1024
                                    i1 = t.matmul(acc[:, 0:512], lh,
                                                  wo_sb[:, base:base + 512],
                                                  start=(hh == 0),
                                                  stop=(hh == 15))
                                    i2 = t.matmul(acc[:, 512:1024], lh,
                                                  wo_sb[:, base + 512:base + 1024],
                                                  start=(hh == 0),
                                                  stop=(hh == 15))
                                    if hh == 15:
                                        i1.then_inc(sCmm.h, 1)
                                        i2.then_inc(sCmm.h, 1)
                                P.add("tensor", fmm)
                            mm_hist[(tg, tb, half)] = sCmm.bump(2)
                        mm_hist[(tg, tb)] = sCmm.n

                        P.wait("vector", sCx[pi], x_rdy)
                        P.wait("vector", sCmm, mm_hist[(tg, tb)])
                        if i >= 2:
                            P.wait("vector", sCxw[pi], sCxw[pi].n)
                        def fev(v, pi=pi):
                            v.tensor_add(xps[pi][:, 0:1024], pSA[:],
                                         xti[pi][:, 0:1024])
                            v.tensor_add(xps[pi][:, 1024:2048], pSB[:],
                                         xti[pi][:, 1024:2048]
                                         ).then_inc(sCev.h, 1)
                        P.add("vector", fev)
                        ev_hist[i] = sCev.bump()

                        P.wait("scalar", sCev, ev_hist[i])
                        def fxw(s, i=i, pi=pi):
                            s.dma_start(xp_d[i * 128:(i + 1) * 128, :],
                                        xps[pi][:]).then_inc(sCxw[pi].h, 16)
                        P.add("scalar", fxw)
                        sCxw[pi].bump(16)
                    mm_hist[(tg, 3)] = sCmm.n
                P.wait("scalar", sCxw[0], sCxw[0].n)
                P.wait("scalar", sCxw[1], sCxw[1].n)

        def plan_P1():
            with ExitStack() as ctx:
                sb = lambda n, sh, dt: ctx.enter_context(nc.sbuf_tensor(n, sh, dt))
                ps = lambda n, sh, dt: ctx.enter_context(nc.psum_tensor(n, sh, dt))
                pSA = ps("P_pA", [128, 1024], F32)
                pSB = ps("P_pB", [128, 1024], F32)
                pT2 = [ps(f"P_pT{p}", [128, 1024], BF16) for p in range(2)]
                xa = [sb(f"P_xa{p}", [128, DM], F32) for p in range(3)]
                xn = [sb(f"P_xn{p}", [128, DM], BF16) for p in range(3)]
                scr1 = sb("P_scr1", [128, DM], BF16)
                scr2 = sb("P_scr2", [128, DM], BF16)
                xn2t = [sb(f"P_x2t{d}", [128, S], BF16) for d in range(16)]
                wgu = [sb(f"P_wgu{p}", [128, 4096], BF16) for p in range(2)]
                gs = [sb(f"P_gs{p}", [128, 512], BF16) for p in range(2)]
                gst = [sb(f"P_gst{p}", [128, 4096], BF16) for p in range(2)]
                stats = mk_stats(ctx, "P", 3)

                mm_hist = {}
                mu_hist = {}
                si_hist = {}
                w_rdy = {}
                slab_rdy = [0] * 8

                def emit_w(hb):
                    par = hb % 2
                    def fw(gp, hb=hb, par=par,
                           need=mm_hist.get((hb - 2, 7), 0)):
                        if need:
                            gp.wait_ge(s1mm.h, need)
                        gp.dma_start(wgu[par][:, 0:2048].rearrange(
                                         "p (dc f) -> p dc f", f=128),
                                     wg_d[hb]).then_inc(s1w[par].h, 16)
                        gp.dma_start(wgu[par][:, 2048:4096].rearrange(
                                         "p (dc f) -> p dc f", f=128),
                                     wu_d[hb]).then_inc(s1w[par].h, 16)
                    P.add("gpsimd", fw)
                    w_rdy[hb] = s1w[par].bump(32)

                def emit_gu(hb, slab):
                    par = hb % 2
                    sp = slab % 2
                    acc = pSA if sp == 0 else pSB
                    if slab == 0:
                        P.wait("tensor", s1w[par], w_rdy[hb])
                    if hb == 0:
                        P.wait("tensor", s1d, slab_rdy[slab])
                    key = (hb, slab - 2) if slab >= 2 else (hb - 1, slab + 6)
                    if key in mu_hist:
                        P.wait("tensor", s1mu, mu_hist[key])
                    if key in si_hist:
                        P.wait("tensor", s1si, si_hist[key])
                    for dc in range(16):
                        def fg(t, par=par, dc=dc, slab=slab, acc=acc):
                            i1 = t.matmul(
                                acc[:, 0:512],
                                wgu[par][:, dc * 128:(dc + 1) * 128],
                                xn2t[dc][:, slab * 512:(slab + 1) * 512],
                                start=(dc == 0), stop=(dc == 15))
                            i2 = t.matmul(
                                acc[:, 512:1024],
                                wgu[par][:, 2048 + dc * 128:
                                         2048 + (dc + 1) * 128],
                                xn2t[dc][:, slab * 512:(slab + 1) * 512],
                                start=(dc == 0), stop=(dc == 15))
                            if dc == 15:
                                i1.then_inc(s1mm.h, 1)
                                i2.then_inc(s1mm.h, 1)
                        P.add("tensor", fg)
                    mm_hist[(hb, slab)] = s1mm.bump(2)

                    P.wait("scalar", s1mm, mm_hist[(hb, slab)])
                    if key in mu_hist:
                        P.wait("scalar", s1mu, mu_hist[key])
                    def fsi(s, sp=sp, acc=acc):
                        s.activation(gs[sp][:], acc[:, 0:512], AF.Silu
                                     ).then_inc(s1si.h, 1)
                    P.add("scalar", fsi)
                    si_hist[(hb, slab)] = si_done = s1si.bump()
                    P.wait("vector", s1si, si_done)
                    if hb >= 2 and slab == 0:
                        P.wait("vector", s1gw[par], s1gw[par].n)
                    def fmu(v, sp=sp, par=par, slab=slab, acc=acc):
                        v.tensor_mul(gst[par][:, slab * 512:(slab + 1) * 512],
                                     gs[sp][:], acc[:, 512:1024]
                                     ).then_inc(s1mu.h, 1)
                    P.add("vector", fmu)
                    mu_hist[(hb, slab)] = s1mu.bump()

                    if slab == 7:
                        P.wait("sync", s1mu, mu_hist[(hb, 7)])
                        def fgw(sy, hb=hb, par=par):
                            src = gst[par][:].rearrange("p (w c) -> p w c",
                                                        c=512)
                            sy.dma_start(gated_d[hb], src
                                         ).then_inc(s1gw[par].h, 16)
                        P.add("sync", fgw)
                        s1gw[par].bump(16)

                emit_w(0)
                emit_w(1)

                L, s_fa1, s_fd1, s_fd2, s_fa2, s_fd3, s_fa3 = mk_ln(
                    xa, xn, stats, scr1, scr2, s1a, s1d, D=3)
                L["sx"] = s1x
                st_tr = {"tr_cnt": 0}
                ev_hist = []

                def s_load(i):
                    pi = i % 3
                    if i >= 3:
                        P.wait("sync", s1a, L["norm"][i - 3])
                    def fl(sy, i=i, pi=pi):
                        sy.dma_start(xa[pi][:], xp_d[i * 128:(i + 1) * 128, :]
                                     ).then_inc(s1x[pi].h, 16)
                    P.add("sync", fl)
                    L["x_rdy"][i] = s1x[pi].bump(16)

                def s_tail(k):
                    if k >= 3:
                        P.wait("scalar", s1p, (k - 3 + 1) * 16)
                    s_fa3(k)
                    pi = k % 3
                    P.wait("tensor", s1a, L["norm"][k])
                    for grp in range(2):
                        tr_cnt = st_tr["tr_cnt"]
                        bk = (tr_cnt // 8) % 2
                        if tr_cnt >= 16:
                            P.wait("tensor", s1d, ev_hist[tr_cnt // 8 - 2])
                        for sl in range(8):
                            dc = grp * 8 + sl
                            def ft(t, pi=pi, dc=dc, sl=sl, bk=bk):
                                t.transpose(pT2[bk][:, sl * 128:(sl + 1) * 128],
                                            xn[pi][:, dc * 128:(dc + 1) * 128],
                                            ident[:]).then_inc(s1p.h, 1)
                            P.add("tensor", ft)
                            s1p.bump()
                            st_tr["tr_cnt"] += 1
                        P.wait("vector", s1p, s1p.n)
                        def fe(v, grp=grp, k=k, bk=bk):
                            ins = None
                            for sl in range(8):
                                dc = grp * 8 + sl
                                ins = v.tensor_copy(
                                    xn2t[dc][:, k * 128:(k + 1) * 128],
                                    pT2[bk][:, sl * 128:(sl + 1) * 128])
                            ins.then_inc(s1d.h, 1)
                        P.add("vector", fe)
                        ev_hist.append(s1d.bump())
                    if k % 4 == 3:
                        slab_rdy[k // 4] = s1d.n
                        emit_gu(0, k // 4)

                for i in range(NTB + 2):
                    if i < NTB:
                        s_load(i)
                        s_fa1(i)
                        s_fd1(i)
                    j = i - 1
                    if 0 <= j < NTB:
                        s_fd2(j)
                        s_fa2(j)
                    k = i - 2
                    if 0 <= k < NTB:
                        s_fd3(k)
                        s_tail(k)

                for hb in range(1, 64):
                    if hb + 1 < 64:
                        emit_w(hb + 1)
                    for slab in range(8):
                        emit_gu(hb, slab)
                P.wait("sync", s1gw[0], s1gw[0].n)
                P.wait("sync", s1gw[1], s1gw[1].n)

        def plan_P2():
            with ExitStack() as ctx:
                sb = lambda n, sh, dt: ctx.enter_context(nc.sbuf_tensor(n, sh, dt))
                ps = lambda n, sh, dt: ctx.enter_context(nc.psum_tensor(n, sh, dt))
                pQ = [ps(f"Q_p{i}", [128, 1024], F32) for i in range(4)]
                wd_sb = sb("Q_wd", [128, 65536], BF16)
                wgb = sb("Q_g", [128, 32768], BF16)
                xpt = [sb(f"Q_xp{p}", [128, 1024], F32) for p in range(2)]
                ost = [sb(f"Q_o{p}", [128, 512], F32) for p in range(2)]

                ev_hist = {}
                ow_hist = {}
                g_hist = {}
                xl_cnt = [0]

                def plan_xp_load(wg, tb, half):
                    pi = xl_cnt[0] % 2
                    prev = ev_hist.get(xl_cnt[0] - 2)
                    if prev:
                        P.wait("sync", s2ev, prev)
                    def fxl(sy, wg=wg, tb=tb, half=half, pi=pi):
                        r0 = (wg * 4 + tb) * 128
                        c0 = half * 1024
                        sy.dma_start(xpt[pi][:],
                                     xp_d[r0:r0 + 128, c0:c0 + 1024]
                                     ).then_inc(s2x[pi].h, 16)
                    P.add("sync", fxl)
                    idx = xl_cnt[0]
                    xl_cnt[0] += 1
                    return pi, s2x[pi].bump(16), idx

                wd_release = {}
                set_ev = [0, 0]
                pcnt = [0]
                for half in range(2):
                    def fwd(gp, half=half,
                            rel=[wd_release.get((half - 1, c), 0)
                                 for c in range(8)]):
                        for c in range(8):
                            if rel[c]:
                                gp.wait_ge(s2mm.h, rel[c])
                            gp.dma_start(
                                wd_sb[:, c * 8192:(c + 1) * 8192].rearrange(
                                    "p (hc f) -> p hc f", f=1024),
                                wd_d[half][:, c * 8:(c + 1) * 8, :]
                            ).then_inc(s2wd[c].h, 16)
                    P.add("gpsimd", fwd)
                    wd_rdy = [s2wd[c].bump(16) for c in range(8)]

                    for wg in range(8):
                        g_rdy = [0] * 4
                        for c16 in range(4):
                            key = (half, wg - 1, c16)
                            if wg == 0 and half == 1:
                                key = (0, 7, c16)
                            if key in g_hist:
                                P.wait("scalar", s2mm, g_hist[key])
                            def fgl(sy, half=half, wg=wg, c16=c16):
                                src = gated_d[c16 * 16:(c16 + 1) * 16, :, wg, :
                                              ].rearrange("h p c -> p h c")
                                dst = wgb[:, c16 * 8192:(c16 + 1) * 8192
                                          ].rearrange("p (h c) -> p h c", c=512)
                                sy.dma_start(dst, src).then_inc(s2g[c16].h, 16)
                            P.add("scalar", fgl)
                            g_rdy[c16] = s2g[c16].bump(16)

                        for ph in range(2):
                            pset = pcnt[0] % 2
                            accs = [[pQ[pset * 2 + tl][:, dp * 512:
                                                       (dp + 1) * 512]
                                     for dp in range(2)] for tl in range(2)]
                            P.wait("tensor", s2wd[0], wd_rdy[0])
                            P.wait("tensor", s2ev, set_ev[pset])
                            for hc in range(64):
                                if hc % 8 == 0 and hc > 0:
                                    P.wait("tensor", s2wd[hc // 8],
                                           wd_rdy[hc // 8])
                                if hc % 16 == 0:
                                    P.wait("tensor", s2g[hc // 16],
                                           g_rdy[hc // 16])
                                ring = hc % 16 == 15 and ph == 1
                                stop = hc == 63
                                wdrel = (wg == 7 and hc % 8 == 7
                                         and ph == 1)
                                for dp in range(2):
                                    for tl in range(2):
                                        tb = ph * 2 + tl
                                        mark = ((ring or wdrel) and dp == 1
                                                and tl == 1) or stop
                                        def fdm(t, hc=hc, tb=tb, tl=tl,
                                                dp=dp, mark=mark, stop=stop,
                                                accs=accs):
                                            lh = wgb[:, hc * 512 + tb * 128:
                                                     hc * 512 + tb * 128 + 128]
                                            rh = wd_sb[:, hc * 1024 + dp * 512:
                                                       hc * 1024 + dp * 512 + 512]
                                            ins = t.matmul(accs[tl][dp], lh, rh,
                                                           start=(hc == 0),
                                                           stop=stop)
                                            if mark:
                                                ins.then_inc(s2mm.h, 1)
                                        P.add("tensor", fdm)
                                        if mark:
                                            s2mm.bump()
                                if ring:
                                    g_hist[(half, wg, hc // 16)] = s2mm.n
                                if wdrel:
                                    wd_release[(half, hc // 8)] = s2mm.n
                            dp_done = s2mm.n

                            loads = [plan_xp_load(wg, ph * 2 + tl, half)
                                     for tl in range(2)]
                            for tl in range(2):
                                pi, x_rdy, idx = loads[tl]
                                P.wait("vector", s2mm, dp_done)
                                P.wait("vector", s2x[pi], x_rdy)
                                for dp in range(2):
                                    po = (2 * tl + dp) % 2
                                    prev_ow = ow_hist.get(po)
                                    if prev_ow:
                                        P.wait("vector", s2ow[po], prev_ow)
                                    def fev(v, tl=tl, pi=pi, po=po, dp=dp,
                                            accs=accs):
                                        v.tensor_add(
                                            ost[po][:], accs[tl][dp],
                                            xpt[pi][:, dp * 512:(dp + 1) * 512]
                                        ).then_inc(s2ev.h, 1)
                                    P.add("vector", fev)
                                    evn = s2ev.bump()
                                    if dp == 1:
                                        ev_hist[idx] = evn
                                    P.wait("sync", s2ev, evn)
                                    def fow(sy, half=half, wg=wg,
                                            tb=ph * 2 + tl, po=po, dp=dp):
                                        r0 = (wg * 4 + tb) * 128
                                        c0 = half * 1024 + dp * 512
                                        sy.dma_start(
                                            out_d[r0:r0 + 128, c0:c0 + 512],
                                            ost[po][:]).then_inc(s2ow[po].h, 16)
                                    P.add("sync", fow)
                                    ow_hist[po] = s2ow[po].bump(16)
                            set_ev[pset] = s2ev.n
                            pcnt[0] += 1

                P.wait("sync", s2ow[0], s2ow[0].n)
                P.wait("sync", s2ow[1], s2ow[1].n)

        plan_setup()
        for eng in Plan.ENGS:
            P.wait(eng, sSET, sSET.n)
            P.wait(eng, sSETd, sSETd.n)
        if "A" in phases:
            plan_A()
        barrier()
        if "B" in phases:
            plan_B()
        barrier()
        if "C" in phases:
            plan_C()
        barrier()
        if "P1" in phases:
            plan_P1()
        barrier()
        if "P2" in phases:
            plan_P2()

        if debug_out:
            scr = {"xnt": (xnt_d, [NE8, 128, 16, 512], BF16, sAw),
                   "yt": (yt_d, [NH, 128, S], BF16, sByw),
                   "xp": (xp_d, [S, DM], F32, sCxw),
                   "gated": (gated_d, [64, 128, 8, 512], BF16, s1gw)}
            sDBG = g_sem("sDBG")
            for nm in debug_out:
                src, shp, dt, dep = scr[nm]
                dst = nc.dram_tensor("dbg_" + nm, shp, dt,
                                     kind="ExternalOutput").ap()
                for d_ in (dep if isinstance(dep, list) else [dep]):
                    P.wait("sync", d_, d_.n)
                def fdb(sy, src=src, dst=dst):
                    sy.dma_start(dst, src).then_inc(sDBG.h, 16)
                P.add("sync", fdb)
                sDBG.bump(16)
            P.wait("sync", sDBG, sDBG.n)

        with nc.Block() as blk:
            blk.sync(lambda e, q=P.q["sync"]: [f(e) for f in q] and None)
            blk.scalar(lambda e, q=P.q["scalar"]: [f(e) for f in q] and None)
            blk.vector(lambda e, q=P.q["vector"]: [f(e) for f in q] and None)
            blk.tensor(lambda e, q=P.q["tensor"]: [f(e) for f in q] and None)
            blk.gpsimd(lambda e, q=P.q["gpsimd"]: [f(e) for f in q] and None)
    return nc



def _prep_weights(Wqkv, Wo, w_gate, w_up, w_down):
    perm = np.concatenate([np.arange(0, 128, 2), np.arange(1, 128, 2)])
    q = Wqkv[:, 0:2048]
    k = Wqkv[:, 2048:4096]
    v = Wqkv[:, 4096:6144]
    wqkv_t = np.empty((NH, 3, 16, 128, 128), np.float32)
    for h in range(NH):
        qh = q[:, h * 128:(h + 1) * 128][:, perm]
        kh = k[:, h * 128:(h + 1) * 128][:, perm]
        vh = v[:, h * 128:(h + 1) * 128]
        wqkv_t[h, 0] = qh.reshape(16, 128, 128)
        wqkv_t[h, 1] = kh.reshape(16, 128, 128)
        wqkv_t[h, 2] = vh.reshape(16, 128, 128)
    wqkv_t = np.ascontiguousarray(wqkv_t.transpose(0, 3, 1, 2, 4))
    wo_t = np.ascontiguousarray(Wo.reshape(16, 128, DM).transpose(1, 0, 2))
    wg_t = np.ascontiguousarray(
        w_gate.reshape(64, 128, 16, 128).transpose(0, 3, 2, 1))
    wu_t = np.ascontiguousarray(
        w_up.reshape(64, 128, 16, 128).transpose(0, 3, 2, 1))
    wdT = np.ascontiguousarray(w_down.T)
    wd_t = np.ascontiguousarray(
        wdT.reshape(64, 128, 2, 1024).transpose(2, 1, 0, 3))
    return wqkv_t, wo_t, wg_t, wu_t, wd_t


def _prep_consts():
    i = np.arange(64, dtype=np.float64)
    inv = 10000.0 ** (-2.0 * i / HD)
    pos = np.arange(S, dtype=np.float64)
    fr = pos[None, :] * inv[:, None]
    c = np.cos(fr)
    s = np.sin(fr)
    cosx = np.concatenate([c, c], 0).astype(np.float32)
    sinx = np.concatenate([-s, s], 0).astype(np.float32)
    pmat = np.zeros((128, 128), np.float32)
    for j in range(64):
        pmat[j + 64, j] = 1.0
        pmat[j, j + 64] = 1.0
    ident = np.eye(128, dtype=np.float32)
    masks = np.zeros((4, 128, 512), np.float32)
    for o in range(4):
        pp = np.arange(128)[:, None]
        cc = np.arange(512)[None, :]
        masks[o] = (cc >= o * 128 + pp).astype(np.float32)
    bf = ml_dtypes.bfloat16
    return (cosx, sinx, pmat.astype(bf), ident.astype(bf), masks.astype(bf))


def kernel(L0_attn_x, L0_attn_Wqkv, L0_attn_Wo, L0_ffn_w_gate, L0_ffn_w_up,
           L0_ffn_w_down):
    x = np.ascontiguousarray(np.asarray(L0_attn_x, dtype=np.float32))
    wqkv_t, wo_t, wg_t, wu_t, wd_t = _prep_weights(
        np.asarray(L0_attn_Wqkv, np.float32),
        np.asarray(L0_attn_Wo, np.float32),
        np.asarray(L0_ffn_w_gate, np.float32),
        np.asarray(L0_ffn_w_up, np.float32),
        np.asarray(L0_ffn_w_down, np.float32))
    cosx, sinx, pmat, ident, masks = _prep_consts()

    nc = build_nc()
    in_maps = []
    for b in range(8):
        in_maps.append(dict(x=np.ascontiguousarray(x[b]), wqkv=wqkv_t,
                            wo=wo_t, wg=wg_t, wu=wu_t, wd=wd_t,
                            cosx=cosx, sinx=sinx, pmat=pmat, ident=ident,
                            masks=masks))
    res = run_bass_kernel_spmd(nc, in_maps, core_ids=list(range(8)))
    out = np.stack([res.results[b]["out"] for b in range(8)], axis=0)
    return out


# version 08
# speedup vs baseline: 1.2312x; 1.2312x over previous
import numpy as np
import ml_dtypes
from contextlib import ExitStack

import concourse.bass as bass
import concourse.mybir as mybir
from concourse.bass_utils import run_bass_kernel_spmd

F32 = mybir.dt.float32
BF16 = mybir.dt.bfloat16
AF = mybir.ActivationFunctionType
ALU = mybir.AluOpType
AX = mybir.AxisListType

S = 4096
DM = 2048
NH = 16
HD = 128
DH = 8192
NTB = S // 128
NE8 = S // 512
SCALE = 1.0 / float(np.sqrt(HD))
EPS = 1e-5


class Sem:
    def __init__(self, h):
        self.h = h
        self.n = 0

    def bump(self, k=1):
        self.n += k
        return self.n


class Plan:
    ENGS = ("sync", "scalar", "vector", "tensor", "gpsimd")

    def __init__(self):
        self.q = {e: [] for e in self.ENGS}

    def add(self, eng, fn):
        self.q[eng].append(fn)

    def wait(self, eng, sem, val):
        if val > 0:
            self.add(eng, lambda e, s=sem.h, v=val: e.wait_ge(s, v))


def build_nc(phases=("A", "B", "C", "P1", "P2"), debug_out=(),
             b_heads=NH, b_flash=True, b_dump=False):
    nc = bass.Bass("TRN2", target_bir_lowering=False)

    x_d = nc.dram_tensor("x", [S, DM], F32, kind="ExternalInput").ap()
    wqkv_d = nc.dram_tensor("wqkv", [NH, 128, 3, 16, 128], F32, kind="ExternalInput").ap()
    wo_d = nc.dram_tensor("wo", [128, 16, DM], F32, kind="ExternalInput").ap()
    wg_d = nc.dram_tensor("wg", [64, 128, 16, 128], F32, kind="ExternalInput").ap()
    wu_d = nc.dram_tensor("wu", [64, 128, 16, 128], F32, kind="ExternalInput").ap()
    wd_d = nc.dram_tensor("wd", [2, 128, 64, 1024], F32, kind="ExternalInput").ap()
    cosx_d = nc.dram_tensor("cosx", [128, S], F32, kind="ExternalInput").ap()
    sinx_d = nc.dram_tensor("sinx", [128, S], F32, kind="ExternalInput").ap()
    pmat_d = nc.dram_tensor("pmat", [128, 128], BF16, kind="ExternalInput").ap()
    ident_d = nc.dram_tensor("ident", [128, 128], BF16, kind="ExternalInput").ap()
    masks_d = nc.dram_tensor("masks", [4, 128, 512], BF16, kind="ExternalInput").ap()

    xnt_d = nc.dram_tensor("xnt", [NE8, 128, 16, 512], BF16, kind="Internal").ap()
    yt_d = nc.dram_tensor("yt", [NH, 128, S], BF16, kind="Internal").ap()
    xp_d = nc.dram_tensor("xp", [S, DM], F32, kind="Internal").ap()
    gated_d = nc.dram_tensor("gated", [64, 128, 8, 512], BF16, kind="Internal").ap()

    out_d = nc.dram_tensor("out", [S, DM], F32, kind="ExternalOutput").ap()

    P = Plan()

    with ExitStack() as gctx:
        def g_sb(name, shape, dt):
            return gctx.enter_context(nc.sbuf_tensor(name, shape, dt))

        def g_ps(name, shape, dt):
            return gctx.enter_context(nc.psum_tensor(name, shape, dt))

        def g_sem(name):
            return Sem(gctx.enter_context(nc.semaphore(name)))


        c_eps = g_sb("c_eps", [128, 1], F32)
        c_nh = g_sb("c_nh", [128, 1], F32)
        c_inv = g_sb("c_inv", [128, 1], F32)
        c_neg1 = g_sb("c_neg1", [128, 1], F32)
        c_scale = g_sb("c_scale", [128, 1], F32)
        onescol = g_sb("onescol", [128, 1], BF16)
        onesrow = g_sb("onesrow", [128, 128], F32)
        ident = g_sb("ident_sb", [128, 128], BF16)
        pmat = g_sb("pmat_sb", [128, 128], BF16)

        sSET = g_sem("sSET")
        sSETd = g_sem("sSETd")
        sBAR = g_sem("sBAR")
        chA = g_sem("chA"); chD = g_sem("chD"); chP = g_sem("chP")
        sDMP = g_sem("sDMP")
        sAx = [g_sem(f"sAx{i}") for i in range(4)]
        sAw = [g_sem("sAw0"), g_sem("sAw1")]
        sAa = g_sem("sAa"); sAd = g_sem("sAd"); sAp = g_sem("sAp")
        sTAB = g_sem("sTAB")
        sBw = [g_sem("sBw0"), g_sem("sBw1")]
        sBx = [g_sem("sBx0"), g_sem("sBx1")]
        sBpe = g_sem("sBpe")
        sBev = g_sem("sBev"); sBrq = g_sem("sBrq"); sBrk = g_sem("sBrk")
        sBvt = g_sem("sBvt"); sBve = g_sem("sBve")
        sBsc = g_sem("sBsc"); sBex = g_sem("sBex"); sBmk = g_sem("sBmk")
        sBavl = g_sem("sBavl"); sBle = g_sem("sBle"); sBlr = g_sem("sBlr")
        sBrb = g_sem("sBrb"); sBrc = g_sem("sBrc"); sByn = g_sem("sByn")
        sBoe = g_sem("sBoe")
        sByw = [g_sem("sByw0"), g_sem("sByw1")]
        sCw = g_sem("sCw")
        sCy = [g_sem("sCy0"), g_sem("sCy1")]
        sCx = [g_sem("sCx0"), g_sem("sCx1")]
        sCmm = g_sem("sCmm"); sCev = g_sem("sCev")
        sCxw = [g_sem("sCxw0"), g_sem("sCxw1")]
        s1x = [g_sem(f"s1x{i}") for i in range(4)]
        s1a = g_sem("s1a"); s1d = g_sem("s1d"); s1p = g_sem("s1p")
        s1w = [g_sem("s1w0"), g_sem("s1w1")]
        s1mm = g_sem("s1mm"); s1si = g_sem("s1si"); s1mu = g_sem("s1mu")
        s1gw = [g_sem("s1gw0"), g_sem("s1gw1")]
        s2wd = [g_sem(f"s2wd{i}") for i in range(8)]
        s2g = [g_sem(f"s2g{i}") for i in range(4)]
        s2x = [g_sem("s2x0"), g_sem("s2x1")]
        s2mm = g_sem("s2mm"); s2ev = g_sem("s2ev")
        s2ow = [g_sem("s2ow0"), g_sem("s2ow1")]

        def plan_setup():
            def f(v):
                v.memset(c_eps[:], EPS)
                v.memset(c_nh[:], -0.5)
                v.memset(c_inv[:], 1.0 / DM)
                v.memset(c_neg1[:], -1.0)
                v.memset(c_scale[:], SCALE)
                v.memset(onescol[:], 1.0)
                v.memset(onesrow[:], 1.0).then_inc(sSET.h, 1)
            P.add("vector", f)
            sSET.bump()

            def fg(gp):
                gp.dma_start(ident[:], ident_d).then_inc(sSETd.h, 16)
                gp.dma_start(pmat[:], pmat_d).then_inc(sSETd.h, 16)
            P.add("gpsimd", fg)
            sSETd.bump(32)

        def barrier():
            base = sBAR.n
            for eng in Plan.ENGS:
                P.add(eng, lambda e: e.nop().then_inc(sBAR.h, 1))
            sBAR.bump(5)
            tgt = base + 5
            for eng in Plan.ENGS:
                P.wait(eng, sBAR, tgt)

        def mk_stats(ctx, tag, n=4):
            out = []
            for p in range(n):
                d = {}
                for nm in ("acc1", "acc2", "mean", "tmp", "var",
                           "rstd", "nmr"):
                    d[nm] = ctx.enter_context(
                        nc.sbuf_tensor(f"{tag}_{nm}{p}", [128, 1], F32))
                out.append(d)
            return out

        def mk_ln(xa, xn, stats, scr1, scr2, sa, sd, D=4):
            M = dict(a_sum={}, d_sq={}, d_var={}, a_rstd={}, d_nmr={},
                     norm={}, x_rdy={})

            def s_fa1(i):
                st = stats[i % D]
                P.wait("scalar", M["sx"][i % D], M["x_rdy"][i])
                if i >= D:
                    P.wait("scalar", sd, M["d_var"][i - D])
                ca = chA.bump()
                def f(s, xa=xa[i % D], a1=st["acc1"], ca=ca):
                    s.memzero(a1[:]).then_inc(chA.h, 1)
                    s.wait_ge(chA.h, ca)
                    s.activation(scr1[:], xa[:], AF.Copy,
                                 accum_out=a1[:, 0:1]).then_inc(sa.h, 1)
                P.add("scalar", f)
                M["a_sum"][i] = sa.bump()

            def s_fd1(i):
                st = stats[i % D]
                P.wait("vector", M["sx"][i % D], M["x_rdy"][i])
                cd = chD.bump()
                def f(v, xa=xa[i % D], a2=st["acc2"], cd=cd):
                    v.memset(a2[:], 0.0).then_inc(chD.h, 1)
                    v.wait_ge(chD.h, cd)
                    v.scalar_tensor_tensor(scr2[:], xa[:], 1.0, xa[:],
                                           ALU.mult, ALU.mult,
                                           accum_out=a2[:, 0:1]
                                           ).then_inc(sd.h, 1)
                P.add("vector", f)
                M["d_sq"][i] = sd.bump()

            def s_fd2(i):
                st = stats[i % D]
                P.wait("vector", sa, M["a_sum"][i])
                if i >= D:
                    P.wait("vector", sa, M["a_rstd"][i - D])
                cd1 = chD.bump(); cd2 = chD.bump()
                def f(v, st=st, cd1=cd1, cd2=cd2):
                    v.tensor_scalar_mul(st["mean"][:], st["acc1"][:],
                                        c_inv[:, 0:1]).then_inc(chD.h, 1)
                    v.wait_ge(chD.h, cd1)
                    v.tensor_mul(st["tmp"][:], st["mean"][:],
                                 st["mean"][:]).then_inc(chD.h, 1)
                    v.wait_ge(chD.h, cd2)
                    v.scalar_tensor_tensor(st["var"][:], st["acc2"][:],
                                           c_inv[:, 0:1], st["tmp"][:],
                                           ALU.mult, ALU.subtract
                                           ).then_inc(sd.h, 1)
                P.add("vector", f)
                M["d_var"][i] = sd.bump()

            def s_fa2(i):
                st = stats[i % D]
                P.wait("scalar", sd, M["d_var"][i])
                if i >= D:
                    P.wait("scalar", sd, M["d_nmr"][i - D])
                ca2 = chA.bump()
                def f(s, st=st, ca2=ca2):
                    s.activation(st["tmp"][:], st["var"][:], AF.Ln,
                                 bias=c_eps[:, 0:1]).then_inc(chA.h, 1)
                    s.wait_ge(chA.h, ca2)
                    s.activation(st["rstd"][:], st["tmp"][:], AF.Exp,
                                 scale=c_nh[:, 0:1]).then_inc(sa.h, 1)
                P.add("scalar", f)
                M["a_rstd"][i] = sa.bump()

            def s_fd3(i):
                st = stats[i % D]
                P.wait("vector", sa, M["a_rstd"][i])
                if i >= D:
                    P.wait("vector", sa, M["norm"][i - D])
                def f(v, st=st):
                    v.scalar_tensor_tensor(st["nmr"][:], st["mean"][:],
                                           c_neg1[:, 0:1], st["rstd"][:],
                                           ALU.mult, ALU.mult
                                           ).then_inc(sd.h, 1)
                P.add("vector", f)
                M["d_nmr"][i] = sd.bump()

            def s_fa3(i):
                st = stats[i % D]
                P.wait("scalar", sd, M["d_nmr"][i])
                def f(s, xa=xa[i % D], xn=xn[i % D], st=st):
                    s.activation(xn[:], xa[:], AF.Identity,
                                 bias=st["nmr"][:, 0:1],
                                 scale=st["rstd"][:, 0:1]).then_inc(sa.h, 1)
                P.add("scalar", f)
                M["norm"][i] = sa.bump()

            return M, s_fa1, s_fd1, s_fd2, s_fa2, s_fd3, s_fa3

        def plan_A():
            with ExitStack() as ctx:
                sb = lambda n, sh, dt: ctx.enter_context(nc.sbuf_tensor(n, sh, dt))
                ps = lambda n, sh, dt: ctx.enter_context(nc.psum_tensor(n, sh, dt))
                xa = [sb(f"A_xa{p}", [128, DM], F32) for p in range(4)]
                xn = [sb(f"A_xn{p}", [128, DM], BF16) for p in range(4)]
                scr1 = sb("A_scr1", [128, DM], F32)
                scr2 = sb("A_scr2", [128, DM], F32)
                stage = [sb(f"A_stg{p}", [128, 8192], BF16) for p in range(2)]
                stats = mk_stats(ctx, "A")
                pT2 = [ps(f"A_pT{p}", [128, 1024], BF16) for p in range(2)]

                L, s_fa1, s_fd1, s_fd2, s_fa2, s_fd3, s_fa3 = mk_ln(
                    xa, xn, stats, scr1, scr2, sAa, sAd)
                L["sx"] = sAx

                ev_done = [0] * NTB
                st_tr = {"tr_cnt": 0}
                ev_cnt_hist = []
                wr_done = [0] * NE8

                def s_load(i):
                    pi = i % 4
                    if i >= 4:
                        P.wait("sync", sAa, L["norm"][i - 4])
                    def fl(sy, i=i, pi=pi):
                        sy.dma_start(xa[pi][:], x_d[i * 128:(i + 1) * 128, :]
                                     ).then_inc(sAx[pi].h, 16)
                    P.add("sync", fl)
                    L["x_rdy"][i] = sAx[pi].bump(16)

                def s_tail(k):
                    if k >= 4:
                        P.wait("scalar", sAp, (k - 4 + 1) * 16)
                    s_fa3(k)
                    pi = k % 4
                    e8 = k // 4
                    slot = k % 4
                    gpar = e8 % 2
                    P.wait("tensor", sAa, L["norm"][k])
                    if e8 >= 2 and slot == 0:
                        P.wait("vector", sAw[gpar], wr_done[e8 - 2])
                    for grp in range(2):
                        tr_cnt = st_tr["tr_cnt"]
                        bk = (tr_cnt // 8) % 2
                        if tr_cnt >= 16:
                            P.wait("tensor", sAd, ev_cnt_hist[tr_cnt // 8 - 2])
                        for sl in range(8):
                            dc = grp * 8 + sl
                            def ft(t, pi=pi, dc=dc, sl=sl, bk=bk):
                                t.transpose(pT2[bk][:, sl * 128:(sl + 1) * 128],
                                            xn[pi][:, dc * 128:(dc + 1) * 128],
                                            ident[:]).then_inc(sAp.h, 1)
                            P.add("tensor", ft)
                            sAp.bump()
                            st_tr["tr_cnt"] += 1
                        P.wait("vector", sAp, sAp.n)
                        def fe(v, gpar=gpar, grp=grp, slot=slot, bk=bk):
                            ins = None
                            for sl in range(8):
                                dc = grp * 8 + sl
                                ins = v.tensor_copy(
                                    stage[gpar][:, dc * 512 + slot * 128:
                                                dc * 512 + slot * 128 + 128],
                                    pT2[bk][:, sl * 128:(sl + 1) * 128])
                            ins.then_inc(sAd.h, 1)
                        P.add("vector", fe)
                        ev_cnt_hist.append(sAd.bump())
                    ev_done[k] = sAd.n
                    if slot == 3:
                        P.wait("sync", sAd, ev_done[k])
                        def fw(sy, e8=e8, gpar=gpar):
                            src = stage[gpar][:].rearrange(
                                "p (dc c) -> p dc c", c=512)
                            sy.dma_start(xnt_d[e8], src).then_inc(sAw[gpar].h, 16)
                        P.add("sync", fw)
                        wr_done[e8] = sAw[gpar].bump(16)

                for i in range(NTB + 2):
                    if i < NTB:
                        s_load(i)
                        s_fa1(i)
                        s_fd1(i)
                    j = i - 1
                    if 0 <= j < NTB:
                        s_fd2(j)
                        s_fa2(j)
                    k = i - 2
                    if 0 <= k < NTB:
                        s_fd3(k)
                        s_tail(k)
                P.wait("sync", sAw[0], sAw[0].n)
                P.wait("sync", sAw[1], sAw[1].n)

        def plan_B():
            with ExitStack() as ctx:
                sb = lambda n, sh, dt: ctx.enter_context(nc.sbuf_tensor(n, sh, dt))
                ps = lambda n, sh, dt: ctx.enter_context(nc.psum_tensor(n, sh, dt))
                pSA = ps("B_pSA", [128, 1024], F32)
                pSB = ps("B_pSB", [128, 1024], F32)
                pO0 = ps("B_pO0", [128, 512], F32)
                pO1 = ps("B_pO1", [128, 512], F32)
                pL = ps("B_pL", [128, 512], F32)
                pTb = ps("B_pTb", [128, 1024], BF16)
                xc = [sb(f"B_xc{p}", [128, 8192], BF16) for p in range(2)]
                wq = [sb(f"B_wq{p}", [128, 6144], BF16) for p in range(2)]
                q_pre = [sb(f"B_qp{p}", [128, 512], BF16) for p in range(2)]
                k_pre = [sb(f"B_kp{p}", [128, 512], BF16) for p in range(2)]
                v_pre = [sb(f"B_vp{p}", [128, 512], BF16) for p in range(2)]
                qsq = [sb(f"B_qsq{p}", [128, 512], BF16) for p in range(2)]
                qsk = [sb(f"B_qsk{p}", [128, 512], BF16) for p in range(2)]
                q_rot = [sb(f"B_qr{p}", [128, S], BF16) for p in range(2)]
                k_rot = [sb(f"B_kr{p}", [128, S], BF16) for p in range(2)]
                v_sb = [sb(f"B_vs{p}", [128, S], BF16) for p in range(2)]
                cosx = sb("B_cosx", [128, S], F32)
                sinx = sb("B_sinx", [128, S], F32)
                tq1 = sb("B_tq1", [128, 512], F32)
                tq2 = sb("B_tq2", [128, 512], F32)
                tk1 = sb("B_tk1", [128, 512], F32)
                tk2 = sb("B_tk2", [128, 512], F32)
                pT = [sb(f"B_pT{p}", [128, 1024], BF16) for p in range(3)]
                masks = sb("B_masks", [128, 2048], BF16)
                oT_sb = [sb(f"B_oT{q}", [128, 1024], F32) for q in range(4)]
                l_sb = [[sb(f"B_l{j}{g}", [128, 512], F32) for g in range(2)]
                        for j in range(2)]
                r_sb = [[sb(f"B_r{j}{g}", [128, 512], F32) for g in range(2)]
                        for j in range(2)]
                rb_sb = [sb(f"B_rb{p}", [128, 512], F32) for p in range(2)]
                y_st = [sb(f"B_y{p}", [128, 512], BF16) for p in range(2)]

                def f0(gp):
                    gp.dma_start(cosx[:], cosx_d).then_inc(sTAB.h, 16)
                    gp.dma_start(sinx[:], sinx_d).then_inc(sTAB.h, 16)
                    gp.dma_start(masks[:].rearrange("p (o c) -> p o c", c=512),
                                 masks_d.rearrange("o p c -> p o c")
                                 ).then_inc(sTAB.h, 16)
                P.add("gpsimd", f0)
                sTAB.bump(48)
                for eng in ("tensor", "vector", "gpsimd", "scalar"):
                    P.wait(eng, sTAB, 48)

                ev_q = {}; ev_k = {}; ev_v = {}; ev_qsq = {}; ev_qsk = {}
                rq_done = {}; rk_done = {}; ve_done = {}
                pe_xc_done = {}
                pe_wq_done = {}
                flash_q_done = {}
                flash_av_done = {}
                wq_req = {}

                def plan_B1(h, interleave=()):
                    interleave = list(interleave)
                    ph2 = h % 2
                    def fw(gp, h=h, ph2=ph2, need=pe_wq_done.get(h - 2, 0)):
                        if need:
                            gp.wait_ge(sBpe.h, need)
                        gp.dma_start(wq[ph2][:].rearrange(
                                         "p (t dc f) -> p t dc f", t=3, dc=16),
                                     wqkv_d[h]).then_inc(sBw[ph2].h, 16)
                    P.add("gpsimd", fw)
                    wq_req[h] = sBw[ph2].bump(16)

                    def plan_qkv(e8):
                        ge = h * NE8 + e8
                        pe2 = ge % 2
                        def fx(sy, e8=e8, pe2=pe2,
                               need=pe_xc_done.get(ge - 2, 0)):
                            if need:
                                sy.wait_ge(sBpe.h, need)
                            dst = xc[pe2][:].rearrange(
                                "p (dc c) -> p dc c", c=512)
                            sy.dma_start(dst, xnt_d[e8]).then_inc(sBx[pe2].h, 16)
                        P.add("sync", fx)
                        x_rdy = sBx[pe2].bump(16)

                        P.wait("tensor", sBx[pe2], x_rdy)
                        P.wait("tensor", sBw[ph2], wq_req[h])
                        if ge > 0:
                            P.wait("tensor", sBev, ev_v.get(ge - 1, 0))
                        P.wait("tensor", sBex, sBex.n)
                        qkv_dst = ((0, pSA, 0), (1, pSA, 512), (2, pSB, 0))
                        for trio, acc, c0 in qkv_dst:
                            for dc in range(16):
                                def fm(t, pe2=pe2, dc=dc, trio=trio,
                                       acc=acc, c0=c0):
                                    ins = t.matmul(
                                        acc[:, c0:c0 + 512],
                                        wq[ph2][:, (trio * 16 + dc) * 128:
                                                (trio * 16 + dc) * 128 + 128],
                                        xc[pe2][:, dc * 512:(dc + 1) * 512],
                                        start=(dc == 0), stop=(dc == 15))
                                    if dc == 15:
                                        ins.then_inc(sBpe.h, 1)
                                P.add("tensor", fm)
                        m_q = sBpe.n + 1
                        m_k = sBpe.n + 2
                        m_v = sBpe.bump(3)
                        qkv_done = m_v
                        pe_xc_done[ge] = m_v
                        if e8 == NE8 - 1:
                            pe_wq_done[h] = m_v

                        P.wait("scalar", sBpe, m_q)
                        if ge >= 2:
                            P.wait("scalar", sBrq, rq_done.get(ge - 2, 0))
                        def fe_q(s, pe2=pe2):
                            s.copy(q_pre[pe2][:], pSA[:, 0:512]).then_inc(sBev.h, 1)
                        P.add("scalar", fe_q)
                        ev_q[ge] = sBev.bump()
                        P.wait("scalar", sBpe, m_k)
                        if ge >= 2:
                            P.wait("scalar", sBrk, rk_done.get(ge - 2, 0))
                        def fe_k(s, pe2=pe2):
                            s.copy(k_pre[pe2][:], pSA[:, 512:1024]).then_inc(sBev.h, 1)
                        P.add("scalar", fe_k)
                        ev_k[ge] = sBev.bump()
                        P.wait("scalar", sBpe, m_v)
                        if ge >= 2:
                            P.wait("scalar", sBvt, 4 * (ge - 1))
                        def fe_v(s, pe2=pe2):
                            s.copy(v_pre[pe2][:], pSB[:, 0:512]).then_inc(sBev.h, 1)
                        P.add("scalar", fe_v)
                        ev_v[ge] = sBev.bump()

                    def plan_tail(e8):
                        ge = h * NE8 + e8
                        pe2 = ge % 2
                        P.wait("tensor", sBev, ev_q[ge])
                        if ge >= 1:
                            P.wait("tensor", sBev, ev_qsq.get(ge - 1, 0))
                        if e8 == 0:
                            P.wait("tensor", sBoe, sBoe.n)
                        def fsq(t, pe2=pe2):
                            t.matmul(pSB[:, 512:1024], pmat[:], q_pre[pe2][:],
                                     start=True, stop=True).then_inc(sBpe.h, 1)
                        P.add("tensor", fsq)
                        P.wait("tensor", sBev, ev_k[ge])
                        if ge >= 1:
                            P.wait("tensor", sBev, ev_qsk.get(ge - 1, 0))
                        def fsk(t, pe2=pe2):
                            t.matmul(pO0[:], pmat[:], k_pre[pe2][:],
                                     start=True, stop=True).then_inc(sBpe.h, 1)
                        P.add("tensor", fsk)
                        qs_done = sBpe.bump(2)
                        P.wait("tensor", sBev, ev_v[ge])
                        if ge >= 1:
                            P.wait("tensor", sBve, ve_done.get(ge - 1, 0))
                        for j in range(4):
                            def ftr(t, pe2=pe2, j=j):
                                t.transpose(pTb[:, j * 128:(j + 1) * 128],
                                            v_pre[pe2][:, j * 128:(j + 1) * 128],
                                            ident[:]).then_inc(sBvt.h, 1)
                            P.add("tensor", ftr)
                        sBvt.bump(4)
                        vt_done = sBvt.n

                        P.wait("scalar", sBpe, qs_done)
                        def feq(s, pe2=pe2):
                            s.copy(qsq[pe2][:], pSB[:, 512:1024]).then_inc(sBev.h, 1)
                            s.copy(qsk[pe2][:], pO0[:]).then_inc(sBev.h, 1)
                        P.add("scalar", feq)
                        ev_qsq[ge] = sBev.n + 1
                        ev_qsk[ge] = sBev.bump(2)

                        P.wait("gpsimd", sBev, ev_qsq[ge])
                        P.wait("gpsimd", sBrq, sBrq.n)
                        if h >= 2 and e8 == 0:
                            P.wait("gpsimd", sBsc, flash_q_done.get(h - 2, 0))
                        cq = chP.bump(2)
                        def frq(gp, pe2=pe2, e8=e8, cq=cq):
                            ts = e8 * 512
                            gp.tensor_mul(tq1[:], q_pre[pe2][:],
                                          cosx[:, ts:ts + 512]).then_inc(chP.h, 1)
                            gp.tensor_mul(tq2[:], qsq[pe2][:],
                                          sinx[:, ts:ts + 512]).then_inc(chP.h, 1)
                            gp.wait_ge(chP.h, cq)
                            gp.tensor_add(q_rot[ph2][:, ts:ts + 512],
                                          tq1[:], tq2[:]).then_inc(sBrq.h, 1)
                        P.add("gpsimd", frq)
                        rq_done[ge] = sBrq.bump()

                        P.wait("gpsimd", sBev, ev_qsk[ge])
                        P.wait("gpsimd", sBrk, sBrk.n)
                        if h >= 2 and e8 == 0:
                            P.wait("gpsimd", sBsc, flash_q_done.get(h - 2, 0))
                        ck = chP.bump(2)
                        def frk(gp, pe2=pe2, e8=e8, ck=ck):
                            ts = e8 * 512
                            gp.tensor_mul(tk1[:], k_pre[pe2][:],
                                          cosx[:, ts:ts + 512]).then_inc(chP.h, 1)
                            gp.tensor_mul(tk2[:], qsk[pe2][:],
                                          sinx[:, ts:ts + 512]).then_inc(chP.h, 1)
                            gp.wait_ge(chP.h, ck)
                            gp.tensor_add(k_rot[ph2][:, ts:ts + 512],
                                          tk1[:], tk2[:]).then_inc(sBrk.h, 1)
                        P.add("gpsimd", frk)
                        rk_done[ge] = sBrk.bump()

                        P.wait("vector", sBvt, vt_done)
                        if h >= 2 and e8 == 0:
                            P.wait("vector", sBavl, flash_av_done.get(h - 2, 0))
                        def fve(v, pe2=pe2, e8=e8):
                            for j in range(4):
                                ins = v.tensor_copy(
                                    v_sb[ph2][:, e8 * 512 + j * 128:
                                              e8 * 512 + j * 128 + 128],
                                    pTb[:, j * 128:(j + 1) * 128])
                            ins.then_inc(sBve.h, 1)
                        P.add("vector", fve)
                        ve_done[ge] = sBve.bump()

                    for e8 in range(NE8):
                        plan_qkv(e8)
                        if e8 >= 1:
                            plan_tail(e8 - 1)
                        if interleave:
                            interleave.pop(0)()
                    plan_tail(NE8 - 1)
                    for em in interleave:
                        em()

                def plan_flash(h):
                    ph2 = h % 2
                    P.wait("tensor", sBrq, rq_done[h * NE8 + NE8 - 1])
                    P.wait("tensor", sBrk, rk_done[h * NE8 + NE8 - 1])
                    P.wait("tensor", sBve, ve_done[h * NE8 + NE8 - 1])
                    P.wait("tensor", sBev, sBev.n)

                    yw_hist = getattr(plan_flash, "_yw_hist", {0: 0, 1: 0})
                    plan_flash._yw_hist = yw_hist
                    le_hist = {}
                    scb = (pSA, pSB)
                    for qq in range(4):
                        n_kt = 8 * qq + 8
                        q0 = qq * 1024
                        exp_hist = {}
                        mk_hist = {}
                        avl_hist = {}
                        P.wait("tensor", sBex, sBex.n)

                        def emit_scores(kt):
                            par = kt % 2
                            both = kt < 8 * qq + 4
                            sc = scb[par]
                            if kt >= 2:
                                P.wait("tensor", sBex, exp_hist[kt - 2])
                            def fs(t, kt=kt, sc=sc, both=both, q0=q0):
                                if both:
                                    t.matmul(sc[:, 0:512],
                                             k_rot[ph2][:, kt * 128:(kt + 1) * 128],
                                             q_rot[ph2][:, q0:q0 + 512],
                                             start=True, stop=True
                                             ).then_inc(sBsc.h, 1)
                                t.matmul(sc[:, 512:1024],
                                         k_rot[ph2][:, kt * 128:(kt + 1) * 128],
                                         q_rot[ph2][:, q0 + 512:q0 + 1024],
                                         start=True, stop=True
                                         ).then_inc(sBsc.h, 1)
                            P.add("tensor", fs)
                            return sBsc.bump(2 if both else 1)

                        def emit_exp(kt, sc_done):
                            par = kt % 2
                            p3 = kt % 3
                            both = kt < 8 * qq + 4
                            sc = scb[par]
                            P.wait("scalar", sBsc, sc_done)
                            if kt >= 3:
                                P.wait("scalar", sBavl, avl_hist[kt - 3])
                            def fx(s, p3=p3, sc=sc, both=both):
                                if both:
                                    s.activation(pT[p3][:], sc[:], AF.Exp,
                                                 scale=c_scale[:, 0:1]
                                                 ).then_inc(sBex.h, 1)
                                else:
                                    s.activation(pT[p3][:, 512:1024],
                                                 sc[:, 512:1024], AF.Exp,
                                                 scale=c_scale[:, 0:1]
                                                 ).then_inc(sBex.h, 1)
                            P.add("scalar", fx)
                            exp_hist[kt] = sBex.bump()

                        def emit_mask(kt):
                            if kt < 8 * qq:
                                return
                            p3 = kt % 3
                            jdiag = (kt - 8 * qq) // 4
                            o = (kt - 8 * qq) % 4
                            P.wait("vector", sBex, exp_hist[kt])
                            def fm(v, p3=p3, jdiag=jdiag, o=o):
                                w = (o + 1) * 128
                                v.tensor_mul(
                                    pT[p3][:, jdiag * 512:jdiag * 512 + w],
                                    pT[p3][:, jdiag * 512:jdiag * 512 + w],
                                    masks[:, o * 512:o * 512 + w]
                                ).then_inc(sBmk.h, 1)
                            P.add("vector", fm)
                            mk_hist[kt] = sBmk.bump()

                        def emit_avl(kt):
                            p3 = kt % 3
                            both = kt < 8 * qq + 4
                            if kt == 0:
                                P.wait("tensor", sBoe, sBoe.n)
                                P.wait("tensor", sBle, sBle.n)
                                P.wait("tensor", sBrc, sBrc.n)
                            if kt in mk_hist:
                                P.wait("tensor", sBmk, mk_hist[kt])
                            else:
                                P.wait("tensor", sBex, exp_hist[kt])
                            js = [0, 1] if both else [1]
                            lasts = {j: (kt == (8 * qq + 4 - 1)) if j == 0
                                     else (kt == n_kt - 1) for j in js}
                            for j in js:
                                pO = pO0 if j == 0 else pO1
                                def fav(t, kt=kt, p3=p3, j=j,
                                        last=lasts[j], pO=pO):
                                    ins = t.matmul(
                                        pO[:],
                                        v_sb[ph2][:, kt * 128:(kt + 1) * 128],
                                        pT[p3][:, j * 512:(j + 1) * 512],
                                        start=(kt == 0), stop=last)
                                    if last:
                                        ins.then_inc(sBavl.h, 1)
                                P.add("tensor", fav)
                                if lasts[j]:
                                    sBavl.bump()
                            for j in js:
                                def fl(t, kt=kt, p3=p3, j=j, last=lasts[j]):
                                    t.matmul(
                                        pL[j * 32:j * 32 + 1, :],
                                        onescol[:],
                                        pT[p3][:, j * 512:(j + 1) * 512],
                                        start=(kt == 0), stop=last
                                    ).then_inc(sBavl.h, 1)
                                P.add("tensor", fl)
                                sBavl.bump()
                            avl_hist[kt] = sBavl.n

                        for kt in range(n_kt):
                            sc_done = emit_scores(kt)
                            emit_exp(kt, sc_done)
                            emit_mask(kt)
                            if kt >= 2:
                                emit_avl(kt - 2)
                        emit_avl(n_kt - 2)
                        emit_avl(n_kt - 1)

                        q_avl = sBavl.n
                        P.wait("scalar", sBavl, q_avl)
                        P.wait("scalar", sBrc, sBrc.n)
                        def fle(s, qq=qq):
                            g, b = qq // 2, (qq % 2) * 32
                            s.copy(l_sb[0][g][b:b + 1, :], pL[0:1, :])
                            s.copy(l_sb[1][g][b:b + 1, :], pL[32:33, :]
                                   ).then_inc(sBle.h, 1)
                        P.add("scalar", fle)
                        le_hist[qq] = sBle.bump()
                        P.wait("scalar", sBavl, q_avl)
                        P.wait("scalar", sByn, sByn.n)
                        def foe0(s, qq=qq):
                            s.copy(oT_sb[qq][:, 0:512], pO0[:]).then_inc(sBoe.h, 1)
                        P.add("scalar", foe0)
                        P.wait("vector", sBavl, q_avl)
                        P.wait("vector", sByn, sByn.n)
                        def foe1(v, qq=qq):
                            v.tensor_copy(oT_sb[qq][:, 512:1024], pO1[:]
                                          ).then_inc(sBoe.h, 1)
                        P.add("vector", foe1)
                        sBoe.bump(2)

                    P.wait("vector", sBle, sBle.n)
                    P.wait("vector", sBrb, sBrb.n)
                    def flr(v):
                        ins = None
                        for qq in range(4):
                            g, b = qq // 2, (qq % 2) * 32
                            ins = v.reciprocal(r_sb[0][g][b:b + 1, :],
                                               l_sb[0][g][b:b + 1, :])
                            ins = v.reciprocal(r_sb[1][g][b:b + 1, :],
                                               l_sb[1][g][b:b + 1, :])
                        ins.then_inc(sBlr.h, 1)
                    P.add("vector", flr)
                    lr_done = sBlr.bump()
                    yn_hist = {}

                    def mk_epi(si):
                        def emit():
                            qq, j = si // 2, si % 2
                            g, b = qq // 2, (qq % 2) * 32
                            P.wait("tensor", sBlr, lr_done)
                            if si >= 1:
                                P.wait("tensor", sBrc, sBrc.n)
                            if si >= 2:
                                P.wait("tensor", sByn, yn_hist[si - 2])
                            def frb(t, j=j, g=g, b=b):
                                t.matmul(pL[:], onesrow[b:b + 1, :],
                                         r_sb[j][g][b:b + 1, :], start=True,
                                         stop=True).then_inc(sBrb.h, 1)
                            P.add("tensor", frb)
                            rbn = sBrb.bump()
                            P.wait("vector", sBrb, rbn)
                            def frc(v, si=si):
                                v.tensor_copy(rb_sb[si % 2][:], pL[:]
                                              ).then_inc(sBrc.h, 1)
                            P.add("vector", frc)
                            rcn = sBrc.bump()
                            P.wait("vector", sBrc, rcn)
                            if yw_hist[si % 2]:
                                P.wait("vector", sByw[si % 2], yw_hist[si % 2])
                            def fyn(v, si=si, qq=qq, j=j):
                                v.tensor_mul(y_st[si % 2][:],
                                             oT_sb[qq][:, j * 512:(j + 1) * 512],
                                             rb_sb[si % 2][:]).then_inc(sByn.h, 1)
                            P.add("vector", fyn)
                            yn_hist[si] = sByn.bump()
                            P.wait("scalar", sByn, yn_hist[si])
                            def fyw(s, h=h, qq=qq, j=j, si=si):
                                c0 = qq * 1024 + j * 512
                                s.dma_start(yt_d[h][:, c0:c0 + 512],
                                            y_st[si % 2][:]).then_inc(
                                                sByw[si % 2].h, 16)
                            P.add("scalar", fyw)
                            yw_hist[si % 2] = sByw[si % 2].bump(16)
                        return emit

                    epi = [mk_epi(si) for si in range(8)]

                    flash_q_done[h] = sBsc.n
                    flash_av_done[h] = sBavl.n
                    return epi

                plan_B1(0)
                pend = []
                for h in range(b_heads):
                    if h + 1 < b_heads:
                        plan_B1(h + 1, interleave=pend)
                    else:
                        for em in pend:
                            em()
                    pend = plan_flash(h) if b_flash else []
                for em in pend:
                    em()
                P.wait("scalar", sByw[0], sByw[0].n)
                P.wait("scalar", sByw[1], sByw[1].n)
                if b_dump:
                    dbg_q = nc.dram_tensor("dbg_qrot", [128, S], BF16,
                                           kind="ExternalOutput").ap()
                    dbg_k = nc.dram_tensor("dbg_krot", [128, S], BF16,
                                           kind="ExternalOutput").ap()
                    dbg_v = nc.dram_tensor("dbg_vsb", [128, S], BF16,
                                           kind="ExternalOutput").ap()
                    hh = (b_heads - 1) % 2
                    P.wait("sync", sBrq, sBrq.n)
                    P.wait("sync", sBrk, sBrk.n)
                    P.wait("sync", sBve, sBve.n)
                    def fdmp(sy, hh=hh):
                        sy.dma_start(dbg_q, q_rot[hh][:]).then_inc(sDMP.h, 16)
                        sy.dma_start(dbg_k, k_rot[hh][:]).then_inc(sDMP.h, 16)
                        sy.dma_start(dbg_v, v_sb[hh][:]).then_inc(sDMP.h, 16)
                    P.add("sync", fdmp)
                    sDMP.bump(48)
                    P.wait("sync", sDMP, sDMP.n)

        def plan_C():
            with ExitStack() as ctx:
                sb = lambda n, sh, dt: ctx.enter_context(nc.sbuf_tensor(n, sh, dt))
                ps = lambda n, sh, dt: ctx.enter_context(nc.psum_tensor(n, sh, dt))
                pSA = ps("C_pA", [128, 1024], F32)
                pSB = ps("C_pB", [128, 1024], F32)
                wo_sb = sb("C_wo", [128, 32768], BF16)
                ytb = [sb(f"C_yt{p}", [128, 8192], BF16) for p in range(2)]
                xti = [sb(f"C_x{p}", [128, DM], F32) for p in range(2)]
                xps = [sb(f"C_xp{p}", [128, DM], F32) for p in range(2)]

                def fw(gp):
                    for c in range(4):
                        gp.dma_start(
                            wo_sb[:, c * 8192:(c + 1) * 8192].rearrange(
                                "p (dc f) -> p dc f", f=2048),
                            wo_d[:, c * 4:(c + 1) * 4, :]).then_inc(sCw.h, 16)
                P.add("gpsimd", fw)
                sCw.bump(64)

                mm_hist = {}
                ev_hist = {}
                for tg in range(8):
                    gpar = tg % 2
                    if tg >= 2:
                        P.wait("sync", sCmm, mm_hist[(tg - 2, 3)])
                    def fy(sy, tg=tg, gpar=gpar):
                        for hh in range(NH):
                            sy.dma_start(
                                ytb[gpar][:, hh * 512:(hh + 1) * 512],
                                yt_d[hh][:, tg * 512:(tg + 1) * 512]
                            ).then_inc(sCy[gpar].h, 16)
                    P.add("sync", fy)
                    y_rdy = sCy[gpar].bump(16 * NH)

                    for tb in range(4):
                        i = tg * 4 + tb
                        pi = i % 2
                        if i >= 2:
                            P.wait("sync", sCev, ev_hist[i - 2])
                        def flx(sy, i=i, pi=pi):
                            sy.dma_start(xti[pi][:],
                                         x_d[i * 128:(i + 1) * 128, :]
                                         ).then_inc(sCx[pi].h, 16)
                        P.add("sync", flx)
                        x_rdy = sCx[pi].bump(16)

                        P.wait("tensor", sCy[gpar], y_rdy)
                        P.wait("tensor", sCw, 64)
                        for half, acc in ((0, pSA), (1, pSB)):
                            if i >= 1:
                                P.wait("tensor", sCev, ev_hist.get(i - 1, 0))
                            for h2 in range(2):
                                for hh in range(NH):
                                    def fmm(t, gpar=gpar, tb=tb, hh=hh,
                                            half=half, acc=acc, h2=h2):
                                        lh = ytb[gpar][:, hh * 512 + tb * 128:
                                                       hh * 512 + tb * 128 + 128]
                                        base = hh * 2048 + half * 1024
                                        ins = t.matmul(
                                            acc[:, h2 * 512:h2 * 512 + 512],
                                            lh,
                                            wo_sb[:, base + h2 * 512:
                                                  base + h2 * 512 + 512],
                                            start=(hh == 0), stop=(hh == 15))
                                        if hh == 15:
                                            ins.then_inc(sCmm.h, 1)
                                    P.add("tensor", fmm)
                            mm_hist[(tg, tb, half)] = sCmm.bump(2)
                        mm_hist[(tg, tb)] = sCmm.n

                        P.wait("vector", sCx[pi], x_rdy)
                        P.wait("vector", sCmm, mm_hist[(tg, tb)])
                        if i >= 2:
                            P.wait("vector", sCxw[pi], sCxw[pi].n)
                        def fev(v, pi=pi):
                            v.tensor_add(xps[pi][:, 0:1024], pSA[:],
                                         xti[pi][:, 0:1024])
                            v.tensor_add(xps[pi][:, 1024:2048], pSB[:],
                                         xti[pi][:, 1024:2048]
                                         ).then_inc(sCev.h, 1)
                        P.add("vector", fev)
                        ev_hist[i] = sCev.bump()

                        P.wait("scalar", sCev, ev_hist[i])
                        def fxw(s, i=i, pi=pi):
                            s.dma_start(xp_d[i * 128:(i + 1) * 128, :],
                                        xps[pi][:]).then_inc(sCxw[pi].h, 16)
                        P.add("scalar", fxw)
                        sCxw[pi].bump(16)
                    mm_hist[(tg, 3)] = sCmm.n
                P.wait("scalar", sCxw[0], sCxw[0].n)
                P.wait("scalar", sCxw[1], sCxw[1].n)

        def plan_P1():
            with ExitStack() as ctx:
                sb = lambda n, sh, dt: ctx.enter_context(nc.sbuf_tensor(n, sh, dt))
                ps = lambda n, sh, dt: ctx.enter_context(nc.psum_tensor(n, sh, dt))
                pSA = ps("P_pA", [128, 1024], F32)
                pSB = ps("P_pB", [128, 1024], F32)
                pT2 = [ps(f"P_pT{p}", [128, 1024], BF16) for p in range(2)]
                xa = [sb(f"P_xa{p}", [128, DM], F32) for p in range(3)]
                xn = [sb(f"P_xn{p}", [128, DM], BF16) for p in range(3)]
                scr1 = sb("P_scr1", [128, DM], BF16)
                scr2 = sb("P_scr2", [128, DM], BF16)
                xn2t = [sb(f"P_x2t{d}", [128, S], BF16) for d in range(16)]
                wgu = [sb(f"P_wgu{p}", [128, 4096], BF16) for p in range(2)]
                gs = [sb(f"P_gs{p}", [128, 512], BF16) for p in range(2)]
                gst = [sb(f"P_gst{p}", [128, 4096], BF16) for p in range(2)]
                stats = mk_stats(ctx, "P", 3)

                mm_hist = {}
                mu_hist = {}
                si_hist = {}
                w_rdy = {}
                slab_rdy = [0] * 8

                def emit_w(hb):
                    par = hb % 2
                    def fw(gp, hb=hb, par=par,
                           need=mm_hist.get((hb - 2, 7), 0)):
                        if need:
                            gp.wait_ge(s1mm.h, need)
                        gp.dma_start(wgu[par][:, 0:2048].rearrange(
                                         "p (dc f) -> p dc f", f=128),
                                     wg_d[hb]).then_inc(s1w[par].h, 16)
                        gp.dma_start(wgu[par][:, 2048:4096].rearrange(
                                         "p (dc f) -> p dc f", f=128),
                                     wu_d[hb]).then_inc(s1w[par].h, 16)
                    P.add("gpsimd", fw)
                    w_rdy[hb] = s1w[par].bump(32)

                def emit_gu(hb, slab):
                    par = hb % 2
                    sp = slab % 2
                    acc = pSA if sp == 0 else pSB
                    if slab == 0:
                        P.wait("tensor", s1w[par], w_rdy[hb])
                    if hb == 0:
                        P.wait("tensor", s1d, slab_rdy[slab])
                    key = (hb, slab - 2) if slab >= 2 else (hb - 1, slab + 6)
                    if key in mu_hist:
                        P.wait("tensor", s1mu, mu_hist[key])
                    if key in si_hist:
                        P.wait("tensor", s1si, si_hist[key])
                    for gu in range(2):
                        for dc in range(16):
                            def fg(t, par=par, dc=dc, slab=slab, acc=acc,
                                   gu=gu):
                                ins = t.matmul(
                                    acc[:, gu * 512:gu * 512 + 512],
                                    wgu[par][:, gu * 2048 + dc * 128:
                                             gu * 2048 + (dc + 1) * 128],
                                    xn2t[dc][:, slab * 512:(slab + 1) * 512],
                                    start=(dc == 0), stop=(dc == 15))
                                if dc == 15:
                                    ins.then_inc(s1mm.h, 1)
                            P.add("tensor", fg)
                    mg = s1mm.n + 1
                    mm_hist[(hb, slab)] = s1mm.bump(2)

                    P.wait("scalar", s1mm, mg)
                    if key in mu_hist:
                        P.wait("scalar", s1mu, mu_hist[key])
                    def fsi(s, sp=sp, acc=acc):
                        s.activation(gs[sp][:], acc[:, 0:512], AF.Silu
                                     ).then_inc(s1si.h, 1)
                    P.add("scalar", fsi)
                    si_hist[(hb, slab)] = si_done = s1si.bump()
                    P.wait("vector", s1si, si_done)
                    P.wait("vector", s1mm, mm_hist[(hb, slab)])
                    if hb >= 2 and slab == 0:
                        P.wait("vector", s1gw[par], s1gw[par].n)
                    def fmu(v, sp=sp, par=par, slab=slab, acc=acc):
                        v.tensor_mul(gst[par][:, slab * 512:(slab + 1) * 512],
                                     gs[sp][:], acc[:, 512:1024]
                                     ).then_inc(s1mu.h, 1)
                    P.add("vector", fmu)
                    mu_hist[(hb, slab)] = s1mu.bump()

                    if slab == 7:
                        P.wait("sync", s1mu, mu_hist[(hb, 7)])
                        def fgw(sy, hb=hb, par=par):
                            src = gst[par][:].rearrange("p (w c) -> p w c",
                                                        c=512)
                            sy.dma_start(gated_d[hb], src
                                         ).then_inc(s1gw[par].h, 16)
                        P.add("sync", fgw)
                        s1gw[par].bump(16)

                emit_w(0)
                emit_w(1)

                L, s_fa1, s_fd1, s_fd2, s_fa2, s_fd3, s_fa3 = mk_ln(
                    xa, xn, stats, scr1, scr2, s1a, s1d, D=3)
                L["sx"] = s1x
                st_tr = {"tr_cnt": 0}
                ev_hist = []

                def s_load(i):
                    pi = i % 3
                    if i >= 3:
                        P.wait("sync", s1a, L["norm"][i - 3])
                    def fl(sy, i=i, pi=pi):
                        sy.dma_start(xa[pi][:], xp_d[i * 128:(i + 1) * 128, :]
                                     ).then_inc(s1x[pi].h, 16)
                    P.add("sync", fl)
                    L["x_rdy"][i] = s1x[pi].bump(16)

                def s_tail(k):
                    if k >= 3:
                        P.wait("scalar", s1p, (k - 3 + 1) * 16)
                    s_fa3(k)
                    pi = k % 3
                    P.wait("tensor", s1a, L["norm"][k])
                    for grp in range(2):
                        tr_cnt = st_tr["tr_cnt"]
                        bk = (tr_cnt // 8) % 2
                        if tr_cnt >= 16:
                            P.wait("tensor", s1d, ev_hist[tr_cnt // 8 - 2])
                        for sl in range(8):
                            dc = grp * 8 + sl
                            def ft(t, pi=pi, dc=dc, sl=sl, bk=bk):
                                t.transpose(pT2[bk][:, sl * 128:(sl + 1) * 128],
                                            xn[pi][:, dc * 128:(dc + 1) * 128],
                                            ident[:]).then_inc(s1p.h, 1)
                            P.add("tensor", ft)
                            s1p.bump()
                            st_tr["tr_cnt"] += 1
                        P.wait("vector", s1p, s1p.n)
                        def fe(v, grp=grp, k=k, bk=bk):
                            ins = None
                            for sl in range(8):
                                dc = grp * 8 + sl
                                ins = v.tensor_copy(
                                    xn2t[dc][:, k * 128:(k + 1) * 128],
                                    pT2[bk][:, sl * 128:(sl + 1) * 128])
                            ins.then_inc(s1d.h, 1)
                        P.add("vector", fe)
                        ev_hist.append(s1d.bump())
                    if k % 4 == 3:
                        slab_rdy[k // 4] = s1d.n
                        emit_gu(0, k // 4)

                for i in range(NTB + 2):
                    if i < NTB:
                        s_load(i)
                        s_fa1(i)
                        s_fd1(i)
                    j = i - 1
                    if 0 <= j < NTB:
                        s_fd2(j)
                        s_fa2(j)
                    k = i - 2
                    if 0 <= k < NTB:
                        s_fd3(k)
                        s_tail(k)

                for hb in range(1, 64):
                    if hb + 1 < 64:
                        emit_w(hb + 1)
                    for slab in range(8):
                        emit_gu(hb, slab)
                P.wait("sync", s1gw[0], s1gw[0].n)
                P.wait("sync", s1gw[1], s1gw[1].n)

        def plan_P2():
            with ExitStack() as ctx:
                sb = lambda n, sh, dt: ctx.enter_context(nc.sbuf_tensor(n, sh, dt))
                ps = lambda n, sh, dt: ctx.enter_context(nc.psum_tensor(n, sh, dt))
                pQ = [ps(f"Q_p{i}", [128, 1024], F32) for i in range(4)]
                wd_sb = sb("Q_wd", [128, 65536], BF16)
                wgb = sb("Q_g", [128, 32768], BF16)
                xpt = [sb(f"Q_xp{p}", [128, 1024], F32) for p in range(2)]
                ost = [sb(f"Q_o{p}", [128, 512], F32) for p in range(2)]

                ev_hist = {}
                ow_hist = {}
                g_hist = {}
                xl_cnt = [0]

                def plan_xp_load(wg, tb, half):
                    pi = xl_cnt[0] % 2
                    prev = ev_hist.get(xl_cnt[0] - 2)
                    if prev:
                        P.wait("sync", s2ev, prev)
                    def fxl(sy, wg=wg, tb=tb, half=half, pi=pi):
                        r0 = (wg * 4 + tb) * 128
                        c0 = half * 1024
                        sy.dma_start(xpt[pi][:],
                                     xp_d[r0:r0 + 128, c0:c0 + 1024]
                                     ).then_inc(s2x[pi].h, 16)
                    P.add("sync", fxl)
                    idx = xl_cnt[0]
                    xl_cnt[0] += 1
                    return pi, s2x[pi].bump(16), idx

                wd_release = {}
                set_ev = [0, 0]
                pcnt = [0]
                for half in range(2):
                    def fwd(gp, half=half,
                            rel=[wd_release.get((half - 1, c), 0)
                                 for c in range(8)]):
                        for c in range(8):
                            if rel[c]:
                                gp.wait_ge(s2mm.h, rel[c])
                            gp.dma_start(
                                wd_sb[:, c * 8192:(c + 1) * 8192].rearrange(
                                    "p (hc f) -> p hc f", f=1024),
                                wd_d[half][:, c * 8:(c + 1) * 8, :]
                            ).then_inc(s2wd[c].h, 16)
                    P.add("gpsimd", fwd)
                    wd_rdy = [s2wd[c].bump(16) for c in range(8)]

                    for wg in range(8):
                        g_rdy = [0] * 4
                        for c16 in range(4):
                            key = (half, wg - 1, c16)
                            if wg == 0 and half == 1:
                                key = (0, 7, c16)
                            if key in g_hist:
                                P.wait("scalar", s2mm, g_hist[key])
                            def fgl(sy, half=half, wg=wg, c16=c16):
                                src = gated_d[c16 * 16:(c16 + 1) * 16, :, wg, :
                                              ].rearrange("h p c -> p h c")
                                dst = wgb[:, c16 * 8192:(c16 + 1) * 8192
                                          ].rearrange("p (h c) -> p h c", c=512)
                                sy.dma_start(dst, src).then_inc(s2g[c16].h, 16)
                            P.add("scalar", fgl)
                            g_rdy[c16] = s2g[c16].bump(16)

                        for ph in range(2):
                            pset = pcnt[0] % 2
                            accs = [[pQ[pset * 2 + tl][:, dp * 512:
                                                       (dp + 1) * 512]
                                     for dp in range(2)] for tl in range(2)]
                            P.wait("tensor", s2wd[0], wd_rdy[0])
                            P.wait("tensor", s2ev, set_ev[pset])
                            for bk16 in range(4):
                                P.wait("tensor", s2wd[2 * bk16],
                                       wd_rdy[2 * bk16])
                                P.wait("tensor", s2wd[2 * bk16 + 1],
                                       wd_rdy[2 * bk16 + 1])
                                P.wait("tensor", s2g[bk16], g_rdy[bk16])
                                for tl in range(2):
                                    for dp in range(2):
                                        tb = ph * 2 + tl
                                        for hco in range(16):
                                            hc = bk16 * 16 + hco
                                            stop = hc == 63
                                            mark = stop or (
                                                ph == 1 and tl == 1
                                                and dp == 1
                                                and (hco == 15 or
                                                     (wg == 7 and
                                                      hco % 8 == 7)))
                                            def fdm(t, hc=hc, tb=tb, tl=tl,
                                                    dp=dp, mark=mark,
                                                    stop=stop, accs=accs):
                                                lh = wgb[:, hc * 512 + tb * 128:
                                                         hc * 512 + tb * 128 + 128]
                                                rh = wd_sb[:, hc * 1024 + dp * 512:
                                                           hc * 1024 + dp * 512 + 512]
                                                ins = t.matmul(accs[tl][dp],
                                                               lh, rh,
                                                               start=(hc == 0),
                                                               stop=stop)
                                                if mark:
                                                    ins.then_inc(s2mm.h, 1)
                                            P.add("tensor", fdm)
                                            if mark:
                                                s2mm.bump()
                                            if (ph == 1 and tl == 1 and dp == 1
                                                    and wg == 7
                                                    and hco % 8 == 7):
                                                wd_release[(half, hc // 8)] = \
                                                    s2mm.n
                                if ph == 1:
                                    g_hist[(half, wg, bk16)] = s2mm.n
                            dp_done = s2mm.n

                            loads = [plan_xp_load(wg, ph * 2 + tl, half)
                                     for tl in range(2)]
                            for tl in range(2):
                                pi, x_rdy, idx = loads[tl]
                                P.wait("vector", s2mm, dp_done)
                                P.wait("vector", s2x[pi], x_rdy)
                                for dp in range(2):
                                    po = (2 * tl + dp) % 2
                                    prev_ow = ow_hist.get(po)
                                    if prev_ow:
                                        P.wait("vector", s2ow[po], prev_ow)
                                    def fev(v, tl=tl, pi=pi, po=po, dp=dp,
                                            accs=accs):
                                        v.tensor_add(
                                            ost[po][:], accs[tl][dp],
                                            xpt[pi][:, dp * 512:(dp + 1) * 512]
                                        ).then_inc(s2ev.h, 1)
                                    P.add("vector", fev)
                                    evn = s2ev.bump()
                                    if dp == 1:
                                        ev_hist[idx] = evn
                                    P.wait("sync", s2ev, evn)
                                    def fow(sy, half=half, wg=wg,
                                            tb=ph * 2 + tl, po=po, dp=dp):
                                        r0 = (wg * 4 + tb) * 128
                                        c0 = half * 1024 + dp * 512
                                        sy.dma_start(
                                            out_d[r0:r0 + 128, c0:c0 + 512],
                                            ost[po][:]).then_inc(s2ow[po].h, 16)
                                    P.add("sync", fow)
                                    ow_hist[po] = s2ow[po].bump(16)
                            set_ev[pset] = s2ev.n
                            pcnt[0] += 1

                P.wait("sync", s2ow[0], s2ow[0].n)
                P.wait("sync", s2ow[1], s2ow[1].n)

        plan_setup()
        for eng in Plan.ENGS:
            P.wait(eng, sSET, sSET.n)
            P.wait(eng, sSETd, sSETd.n)
        if "A" in phases:
            plan_A()
        barrier()
        if "B" in phases:
            plan_B()
        barrier()
        if "C" in phases:
            plan_C()
        barrier()
        if "P1" in phases:
            plan_P1()
        barrier()
        if "P2" in phases:
            plan_P2()

        if debug_out:
            scr = {"xnt": (xnt_d, [NE8, 128, 16, 512], BF16, sAw),
                   "yt": (yt_d, [NH, 128, S], BF16, sByw),
                   "xp": (xp_d, [S, DM], F32, sCxw),
                   "gated": (gated_d, [64, 128, 8, 512], BF16, s1gw)}
            sDBG = g_sem("sDBG")
            for nm in debug_out:
                src, shp, dt, dep = scr[nm]
                dst = nc.dram_tensor("dbg_" + nm, shp, dt,
                                     kind="ExternalOutput").ap()
                for d_ in (dep if isinstance(dep, list) else [dep]):
                    P.wait("sync", d_, d_.n)
                def fdb(sy, src=src, dst=dst):
                    sy.dma_start(dst, src).then_inc(sDBG.h, 16)
                P.add("sync", fdb)
                sDBG.bump(16)
            P.wait("sync", sDBG, sDBG.n)

        with nc.Block() as blk:
            blk.sync(lambda e, q=P.q["sync"]: [f(e) for f in q] and None)
            blk.scalar(lambda e, q=P.q["scalar"]: [f(e) for f in q] and None)
            blk.vector(lambda e, q=P.q["vector"]: [f(e) for f in q] and None)
            blk.tensor(lambda e, q=P.q["tensor"]: [f(e) for f in q] and None)
            blk.gpsimd(lambda e, q=P.q["gpsimd"]: [f(e) for f in q] and None)
    return nc



def _prep_weights(Wqkv, Wo, w_gate, w_up, w_down):
    perm = np.concatenate([np.arange(0, 128, 2), np.arange(1, 128, 2)])
    q = Wqkv[:, 0:2048]
    k = Wqkv[:, 2048:4096]
    v = Wqkv[:, 4096:6144]
    wqkv_t = np.empty((NH, 3, 16, 128, 128), np.float32)
    for h in range(NH):
        qh = q[:, h * 128:(h + 1) * 128][:, perm]
        kh = k[:, h * 128:(h + 1) * 128][:, perm]
        vh = v[:, h * 128:(h + 1) * 128]
        wqkv_t[h, 0] = qh.reshape(16, 128, 128)
        wqkv_t[h, 1] = kh.reshape(16, 128, 128)
        wqkv_t[h, 2] = vh.reshape(16, 128, 128)
    wqkv_t = np.ascontiguousarray(wqkv_t.transpose(0, 3, 1, 2, 4))
    wo_t = np.ascontiguousarray(Wo.reshape(16, 128, DM).transpose(1, 0, 2))
    wg_t = np.ascontiguousarray(
        w_gate.reshape(64, 128, 16, 128).transpose(0, 3, 2, 1))
    wu_t = np.ascontiguousarray(
        w_up.reshape(64, 128, 16, 128).transpose(0, 3, 2, 1))
    wdT = np.ascontiguousarray(w_down.T)
    wd_t = np.ascontiguousarray(
        wdT.reshape(64, 128, 2, 1024).transpose(2, 1, 0, 3))
    return wqkv_t, wo_t, wg_t, wu_t, wd_t


def _prep_consts():
    i = np.arange(64, dtype=np.float64)
    inv = 10000.0 ** (-2.0 * i / HD)
    pos = np.arange(S, dtype=np.float64)
    fr = pos[None, :] * inv[:, None]
    c = np.cos(fr)
    s = np.sin(fr)
    cosx = np.concatenate([c, c], 0).astype(np.float32)
    sinx = np.concatenate([-s, s], 0).astype(np.float32)
    pmat = np.zeros((128, 128), np.float32)
    for j in range(64):
        pmat[j + 64, j] = 1.0
        pmat[j, j + 64] = 1.0
    ident = np.eye(128, dtype=np.float32)
    masks = np.zeros((4, 128, 512), np.float32)
    for o in range(4):
        pp = np.arange(128)[:, None]
        cc = np.arange(512)[None, :]
        masks[o] = (cc >= o * 128 + pp).astype(np.float32)
    bf = ml_dtypes.bfloat16
    return (cosx, sinx, pmat.astype(bf), ident.astype(bf), masks.astype(bf))


def kernel(L0_attn_x, L0_attn_Wqkv, L0_attn_Wo, L0_ffn_w_gate, L0_ffn_w_up,
           L0_ffn_w_down):
    x = np.ascontiguousarray(np.asarray(L0_attn_x, dtype=np.float32))
    wqkv_t, wo_t, wg_t, wu_t, wd_t = _prep_weights(
        np.asarray(L0_attn_Wqkv, np.float32),
        np.asarray(L0_attn_Wo, np.float32),
        np.asarray(L0_ffn_w_gate, np.float32),
        np.asarray(L0_ffn_w_up, np.float32),
        np.asarray(L0_ffn_w_down, np.float32))
    cosx, sinx, pmat, ident, masks = _prep_consts()

    nc = build_nc()
    in_maps = []
    for b in range(8):
        in_maps.append(dict(x=np.ascontiguousarray(x[b]), wqkv=wqkv_t,
                            wo=wo_t, wg=wg_t, wu=wu_t, wd=wd_t,
                            cosx=cosx, sinx=sinx, pmat=pmat, ident=ident,
                            masks=masks))
    res = run_bass_kernel_spmd(nc, in_maps, core_ids=list(range(8)))
    out = np.stack([res.results[b]["out"] for b in range(8)], axis=0)
    return out
